# version 01
# baseline (speedup 1.0000x reference)
DEBUG = False
OUT_KEY = "y_out"
FFN_STAGE = 9
import numpy as np
import ml_dtypes
from contextlib import ExitStack

import concourse.bass as bass
import concourse.mybir as mybir
from concourse.bass_utils import run_bass_kernel_spmd
from concourse.bass_core_rust import IndirectOffsetOnAxis

F32 = mybir.dt.float32
BF16 = mybir.dt.bfloat16
I32 = mybir.dt.int32
U32 = mybir.dt.uint32
AF = mybir.ActivationFunctionType
ALU = mybir.AluOpType
AX = mybir.AxisListType

P = 128
D = 2048
NT_OWN = 16
NT_CTX = 24
HALO = 1024
HD = 64
LN_EPS = 1e-5

HEAD_ORDER = []
for _j in range(4):
    for _i in range(4):
        HEAD_ORDER += [8 * _j + _i, 8 * _j + 4 + _i]
PAIR_HEADS = [(HEAD_ORDER[2 * m], HEAD_ORDER[2 * m + 1]) for m in range(16)]


class Ctr:
    def __init__(self, sem):
        self.sem = sem
        self.v = 0

    def inc(self, inst, n=1):
        inst.then_inc(self.sem, n)
        self.v += n


class B:
    def __init__(self, nc, ctx):
        self.nc = nc
        self.ctx = ctx
        e = ctx.enter_context
        self.cS = Ctr(e(nc.semaphore("semS")))
        self.cG = Ctr(e(nc.semaphore("semG")))
        self.cP = Ctr(e(nc.semaphore("semP")))
        self.cV = Ctr(e(nc.semaphore("semV")))
        self.cA = Ctr(e(nc.semaphore("semA")))
        self.all = [self.cS, self.cG, self.cP, self.cV, self.cA]
        self.V = StrictEng(nc.vector, self.cV)
        self.A = StrictEng(nc.scalar, self.cA)

    def sb(self, name, shape, dtype):
        return self.ctx.enter_context(self.nc.sbuf_tensor(name, shape, dtype))

    def ps(self, name, shape, dtype=F32):
        return self.ctx.enter_context(self.nc.psum_tensor(name, shape, dtype))

    def wait(self, eng, *ctrs):
        for c in ctrs:
            eng.wait_ge(c.sem, c.v)

    def sync_all(self, eng):
        for c in self.all:
            eng.wait_ge(c.sem, c.v)


_WRAP_OPS = {
    "tensor_copy", "tensor_tensor", "tensor_add", "tensor_sub", "tensor_mul",
    "tensor_max", "tensor_scalar", "tensor_single_scalar", "tensor_scalar_add",
    "tensor_scalar_sub", "tensor_scalar_mul", "tensor_scalar_max",
    "tensor_scalar_min", "tensor_relu", "tensor_reduce", "reduce_max",
    "reduce_sum", "scalar_tensor_tensor", "tensor_tensor_scan", "memset",
    "memzero", "select", "copy_predicated", "max", "max_index", "match_replace",
    "reciprocal", "transpose", "bn_stats", "bn_aggr", "stream_shuffle",
    "activation", "sign", "square", "sqrt", "mul", "copy", "add", "iota",
    "engine_nop",
}


class StrictEng:

    def __init__(self, eng, ctr):
        self._eng = eng
        self._ctr = ctr

    def __getattr__(self, name):
        f = getattr(self._eng, name)
        if name not in _WRAP_OPS:
            return f

        def wrapped(*a, **k):
            inst = f(*a, **k)
            self._ctr.inc(inst)
            self._eng.wait_ge(self._ctr.sem, self._ctr.v)
            return inst
        return wrapped


def emit_ln(b, x_sb, sq_scr, stat, ln_out):
    V, A = b.V, b.A
    sx, sxx, mean, varb, rstd, bias, tmp = (stat[:, i:i + 1] for i in range(7))
    V.memset(stat[:, 0:2], 0.0)
    b.sync_all(b.nc.scalar)
    A.activation(out=sq_scr[:], in_=x_sb[:], func=AF.Square, accum_out=sxx)
    A.activation(out=sq_scr[:], in_=x_sb[:], func=AF.Copy, accum_out=sx)
    b.wait(b.nc.vector, b.cA)
    V.tensor_scalar_mul(mean, sx, 1.0 / D)
    V.tensor_scalar_mul(tmp, sxx, 1.0 / D)
    V.scalar_tensor_tensor(out=varb, in0=mean, scalar=-1.0, in1=mean,
                           op0=ALU.mult, op1=ALU.mult)
    V.tensor_add(varb, varb, tmp)
    V.tensor_scalar_add(varb, varb, LN_EPS)
    b.wait(b.nc.scalar, b.cV)
    A.activation(out=rstd, in_=varb, func=AF.Ln)
    A.activation(out=rstd, in_=rstd, func=AF.Exp, scale=-0.5)
    b.wait(b.nc.vector, b.cA)
    V.scalar_tensor_tensor(out=bias, in0=mean, scalar=-1.0, in1=rstd,
                           op0=ALU.mult, op1=ALU.mult)
    b.wait(b.nc.scalar, b.cV)
    A.activation(out=ln_out[:], in_=x_sb[:], func=AF.Identity,
                 bias=bias, scale=rstd)


def emit_ffn(b, env, x_in, x_out, x_in_off):
    nc = b.nc
    T, V, A, S, G = env["T"], env["V"], env["A"], env["S"], env["G"]
    (wqf_sb, keys_sb, iota64_sb, lnf_sb, _unused, stat_sb, xb_sb, qf_sb,
     sims_sb, t8_sb, i8u_sb, i8f_sb, cand_sb, emat_sb, f8_sb, c8u_sb, c8f_sb,
     oh_sb, esel_sb, eidx_sb, fsr_sb, hid_sb, coeff_sb, gdn_sb, gup_sb,
     acc_sb, junk_sb) = env["ffn_bufs"]
    ps_proj = env["ps_proj"]
    ps_tr2 = env["ps_tr2"]
    e_down, e_up = env["e_down"], env["e_up"]

    for i in range(NT_OWN):
        rows = slice(x_in_off + i * P, x_in_off + (i + 1) * P)
        orows = slice(i * P, (i + 1) * P)
        b.sync_all(S)
        b.cS.inc(S.dma_start(out=xb_sb[:], in_=x_in[rows, :]), 16)
        b.wait(A, b.cS)
        b.wait(V, b.cS)
        if FFN_STAGE < 1:
            V.tensor_copy(acc_sb[:], xb_sb[:])
            b.sync_all(S)
            b.cS.inc(S.dma_start(out=x_out[orows, :], in_=acc_sb[:]), 16)
            continue
        emit_ln(b, xb_sb, lnf_sb, stat_sb, lnf_sb)
        b.wait(V, b.cA)
        if FFN_STAGE < 1.5:
            V.tensor_copy(acc_sb[:], xb_sb[:])
            b.sync_all(S)
            b.cS.inc(S.dma_start(out=x_out[orows, :], in_=acc_sb[:]), 16)
            continue
        for cb in range(4):
            pp = ps_proj[cb % 2]
            b.sync_all(T)
            for f in range(16):
                if cb == 0:
                    b.cP.inc(T.transpose(ps_tr2[f % 2], lnf_sb[:, f * P:(f + 1) * P],
                                         env["ident_f32"][:]))
                    b.wait(V, b.cP)
                    V.tensor_copy(env["lnT_sb"][:, f, :], ps_tr2[f % 2])
                    b.wait(T, b.cV)
            for f in range(16):
                b.cP.inc(T.matmul(pp, wqf_sb[:, f, cb * P:(cb + 1) * P],
                                  env["lnT_sb"][:, f, :], start=(f == 0), stop=(f == 15)))
            b.wait(V, b.cP)
            V.tensor_copy(qf_sb[:, cb, :], pp)
            b.wait(T, b.cV)
        ps_sims = env["ps_sims"]
        for h in range(4):
            b.sync_all(T)
            T.matmul(ps_sims[0], qf_sb[0:64, h, :], keys_sb[0:64, 2 * h, :],
                     start=True, stop=True, tile_position=(0, 0))
            b.cP.inc(T.matmul(ps_sims[1], qf_sb[64:128, h, :], keys_sb[64:128, 2 * h + 1, :],
                              start=True, stop=True, tile_position=(64, 0)))
            b.wait(V, b.cP)
            V.tensor_copy(sims_sb[:, 2 * h, :], ps_sims[0])
            V.tensor_copy(sims_sb[:, 2 * h + 1, :], ps_sims[1])
            b.wait(T, b.cV)
        if FFN_STAGE < 2:
            V.tensor_copy(acc_sb[:], xb_sb[:])
            b.sync_all(S)
            b.cS.inc(S.dma_start(out=x_out[orows, :], in_=acc_sb[:]), 16)
            continue
        for hs in range(8):
            V.max(out=t8_sb[:, hs, :], in_=sims_sb[:, hs, :])
            V.max_index(out=i8u_sb[:, hs, :], in_max=t8_sb[:, hs, :],
                        in_values=sims_sb[:, hs, :])
        V.tensor_copy(i8f_sb[:], i8u_sb[:])
        for h in range(4):
            s1 = t8_sb[:, 2 * h, :]
            s2 = t8_sb[:, 2 * h + 1, :]
            i1f = i8f_sb[:, 2 * h, :]
            i2f = i8f_sb[:, 2 * h + 1, :]
            V.tensor_scalar_mul(junk_sb[:, 0:8], i1f, 128.0)
            for a in range(8):
                V.tensor_scalar(cand_sb[:, a * 8:(a + 1) * 8], s2, s1[:, a:a + 1],
                                None, op0=ALU.add)
                V.tensor_scalar(emat_sb[:, a * 8:(a + 1) * 8], i2f,
                                junk_sb[:, a:a + 1], None, op0=ALU.add)
            V.max(out=f8_sb[:, h, :], in_=cand_sb[:])
            V.max_index(out=c8u_sb[:], in_max=f8_sb[:, h, :], in_values=cand_sb[:])
            V.tensor_copy(c8f_sb[:], c8u_sb[:])
            for r in range(8):
                V.tensor_scalar(oh_sb[:], iota64_sb[:], c8f_sb[:, r:r + 1], None,
                                op0=ALU.is_equal)
                V.scalar_tensor_tensor(out=junk_sb[:, 8:72], in0=oh_sb[:], scalar=1.0,
                                       in1=emat_sb[:], op0=ALU.mult, op1=ALU.mult,
                                       accum_out=esel_sb[:, h * 8 + r:h * 8 + r + 1])
            V.tensor_scalar_max(fsr_sb[:, h, :], f8_sb[:, h, :], 0.0)
        V.tensor_scalar_max(esel_sb[:], esel_sb[:], 0.0)
        V.tensor_scalar_min(esel_sb[:], esel_sb[:], 16383.0)
        V.tensor_copy(eidx_sb[:], esel_sb[:])
        if env.get("dbg") is not None and i == 0:
            d = env["dbg"]
            b.sync_all(S)
            b.cS.inc(S.dma_start(out=d["sims"], in_=sims_sb[:]), 16)
            b.cS.inc(S.dma_start(out=d["t8"], in_=t8_sb[:]), 16)
            b.cS.inc(S.dma_start(out=d["i8u"], in_=i8u_sb[:]), 16)
            b.cS.inc(S.dma_start(out=d["eidx"], in_=eidx_sb[:]), 16)
            b.cS.inc(S.dma_start(out=d["f8"], in_=f8_sb[:]), 16)
        if FFN_STAGE < 3:
            V.tensor_copy(acc_sb[:], xb_sb[:])
            b.sync_all(S)
            b.cS.inc(S.dma_start(out=x_out[orows, :], in_=acc_sb[:]), 16)
            continue
        V.memset(hid_sb[:], 0.0)
        for s in range(32):
            b.sync_all(G)
            b.cG.inc(G.indirect_dma_start(
                out=gdn_sb[:], out_offset=None, in_=e_down[0:P, :],
                in_offset=IndirectOffsetOnAxis(ap=eidx_sb[:, s:s + 1], axis=0)), 16)
            b.wait(V, b.cG)
            V.scalar_tensor_tensor(out=junk2_dummy(env), in0=gdn_sb[:], scalar=1.0,
                                   in1=lnf_sb[:], op0=ALU.mult, op1=ALU.mult,
                                   accum_out=hid_sb[:, s:s + 1])
        if FFN_STAGE < 4:
            V.tensor_copy(acc_sb[:], xb_sb[:])
            b.sync_all(S)
            b.cS.inc(S.dma_start(out=x_out[orows, :], in_=acc_sb[:]), 16)
            continue
        b.wait(A, b.cV)
        A.activation(out=hid_sb[:], in_=hid_sb[:], func=AF.Silu)
        b.wait(V, b.cA)
        V.tensor_mul(coeff_sb[:], hid_sb[:], fsr_sb[:].rearrange("p h r -> p (h r)"))
        V.memset(acc_sb[:], 0.0)
        for s in range(32):
            b.sync_all(G)
            b.cG.inc(G.indirect_dma_start(
                out=gup_sb[:], out_offset=None, in_=e_up[0:P, :],
                in_offset=IndirectOffsetOnAxis(ap=eidx_sb[:, s:s + 1], axis=0)), 16)
            b.wait(V, b.cG)
            V.scalar_tensor_tensor(out=acc_sb[:], in0=gup_sb[:], scalar=coeff_sb[:, s:s + 1],
                                   in1=acc_sb[:], op0=ALU.mult, op1=ALU.add)
        V.tensor_add(acc_sb[:], acc_sb[:], xb_sb[:])
        b.sync_all(S)
        b.cS.inc(S.dma_start(out=x_out[orows, :], in_=acc_sb[:]), 16)


def junk2_dummy(env):
    return env["junk2"][:]


def build_l1():
    nc = bass.Bass("TRN2", target_bir_lowering=False)
    dt = nc.dram_tensor
    x0h = dt("x0h", [NT_CTX * P, D], F32, kind="ExternalInput").ap()
    wqT = dt("wqT", [D, D], F32, kind="ExternalInput").ap()
    wkT = dt("wkT", [D, 512], F32, kind="ExternalInput").ap()
    wvT = dt("wvT", [D, 512], F32, kind="ExternalInput").ap()
    woT = dt("woT", [D, D], F32, kind="ExternalInput").ap()
    cuq = dt("cuq", [1, 4], I32, kind="ExternalInput").ap()
    identity = dt("identity", [P, P], BF16, kind="ExternalInput").ap()
    identity32 = dt("identity32", [P, P], F32, kind="ExternalInput").ap()
    tri0 = dt("tri0", [P, P], BF16, kind="ExternalInput").ap()
    tri8 = dt("tri8", [P, P], BF16, kind="ExternalInput").ap()
    cmask_in = dt("cmask_in", [P, NT_CTX], F32, kind="ExternalInput").ap()
    iotaJ = dt("iotaJ", [P, NT_CTX], F32, kind="ExternalInput").ap()
    iotaI = dt("iotaI", [P, NT_OWN], F32, kind="ExternalInput").ap()
    onescol = dt("onescol", [P, 1], F32, kind="ExternalInput").ap()
    onesrow = dt("onesrow", [1, P], F32, kind="ExternalInput").ap()
    x1_out = dt("x1_out", [NT_OWN * P, D], F32, kind="ExternalOutput").ap()
    x2_out = dt("x2_out", [NT_OWN * P, D], F32, kind="ExternalOutput").ap()
    wqf = dt("wqf", [D, 512], F32, kind="ExternalInput").ap()
    keysT = dt("keysT", [64, 8, P], F32, kind="ExternalInput").ap()
    e_down = dt("e_down", [16384, D], F32, kind="ExternalInput").ap()
    e_up = dt("e_up", [16384, D], F32, kind="ExternalInput").ap()
    iota64 = dt("iota64", [P, 64], F32, kind="ExternalInput").ap()
    wk3T = dt("wk3T", [D, 512], F32, kind="ExternalInput").ap()
    bk3 = dt("bk3", [1, 512], F32, kind="ExternalInput").ap()
    wv3T = dt("wv3T", [D, 512], F32, kind="ExternalInput").ap()
    bv3 = dt("bv3", [1, 512], F32, kind="ExternalInput").ap()
    wmb3T = dt("wmb3T", [D, D], F32, kind="ExternalInput").ap()
    bmb3 = dt("bmb3", [1, D], F32, kind="ExternalInput").ap()
    meml = dt("meml", [16, 6, P, 129], F32, kind="ExternalInput").ap()
    delta_out = dt("delta_out", [16, 6, P, P], F32, kind="ExternalOutput").ap()
    dnorm_out = dt("dnorm_out", [6, P, 16], F32, kind="ExternalOutput").ap()
    mb3_dram = dt("mb3_dram", [NT_OWN * P, D], BF16, kind="Internal").ap()
    dramkind = "ExternalOutput" if DEBUG else "Internal"
    lnT_dram = dt("lnT_dram", [16, P, NT_CTX * P], BF16, kind=dramkind).ap()
    q_dram = dt("q_dram", [16, P, NT_OWN * P], BF16, kind=dramkind).ap()
    aT_dram = dt("aT_dram", [16, P, NT_OWN * P], BF16, kind=dramkind).ap()
    if DEBUG:
        dbg_b = dict(
            k3=dt("dbgb_k3", [P, 4, D], BF16, kind="ExternalOutput").ap(),
            v3=dt("dbgb_v3", [P, 16, 512], BF16, kind="ExternalOutput").ap(),
            dptok=dt("dbgb_dptok", [P, 16, 768], BF16, kind="ExternalOutput").ap(),
            snorm=dt("dbgb_snorm", [P, 16], F32, kind="ExternalOutput").ap(),
            rnorm=dt("dbgb_rnorm", [P, 16], F32, kind="ExternalOutput").ap(),
            wmv=dt("dbgb_wmv", [P, 16, P], BF16, kind="ExternalOutput").ap(),
            pvall=dt("dbgb_pvall", [P, 16, P], BF16, kind="ExternalOutput").ap(),
            mbh=dt("dbgb_mbh", [P, 16, P], BF16, kind="ExternalOutput").ap(),
            numall=dt("dbgb_numall", [P, 16, P], BF16, kind="ExternalOutput").ap(),
            denall=dt("dbgb_denall", [P, 16, 2], F32, kind="ExternalOutput").ap(),
            coefs=dt("dbgb_coefs", [P, 16, 4], BF16, kind="ExternalOutput").ap(),
            dpfp=dt("dbgb_dpfp", [P, 6, D], BF16, kind="ExternalOutput").ap(),
        )
        dbg_f = dict(
            sims=dt("dbgf_sims", [P, 8, P], F32, kind="ExternalOutput").ap(),
            t8=dt("dbgf_t8", [P, 8, 8], F32, kind="ExternalOutput").ap(),
            i8u=dt("dbgf_i8u", [P, 8, 8], U32, kind="ExternalOutput").ap(),
            eidx=dt("dbgf_eidx", [P, 32], U32, kind="ExternalOutput").ap(),
            f8=dt("dbgf_f8", [P, 4, 8], F32, kind="ExternalOutput").ap(),
        )
        dbg_norm = dt("dbg_norm", [1, 64], F32, kind="ExternalOutput").ap()
        dbg_negc = dt("dbg_negc", [P, 32], F32, kind="ExternalOutput").ap()
        dbg_kpair = dt("dbg_kpair", [P, 4, NT_CTX * P], BF16, kind="ExternalOutput").ap()
        dbg_vaug = dt("dbg_vaug", [P, 8, NT_CTX, 130], BF16, kind="ExternalOutput").ap()
        dbg_probs = dt("dbg_probs", [2, P, 9 * P], BF16, kind="ExternalOutput").ap()

    with ExitStack() as ctx:
        b = B(nc, ctx)
        T, V, A, S, G = nc.tensor, b.V, b.A, nc.sync, nc.gpsimd

        ident_sb = b.sb("ident_sb", [P, P], BF16)
        ident_f32 = b.sb("ident_f32", [P, P], F32)
        tri0_sb = b.sb("tri0_sb", [P, P], BF16)
        tri8_sb = b.sb("tri8_sb", [P, P], BF16)
        cmask_sb = b.sb("cmask_sb", [P, NT_CTX], F32)
        bprime_sb = b.sb("bprime_sb", [P, NT_CTX], F32)
        iotaJ_sb = b.sb("iotaJ_sb", [P, NT_CTX], F32)
        iotaI_sb = b.sb("iotaI_sb", [P, NT_OWN], F32)
        aq_sb = b.sb("aq_sb", [P, NT_OWN], F32)
        aqU_sb = b.sb("aqU_sb", [P, NT_OWN], mybir.dt.uint8)
        ones_sb = b.sb("ones_sb", [P, 1], F32)
        onesrow_sb = b.sb("onesrow_sb", [1, P], F32)
        cu_sb = b.sb("cu_sb", [1, 4], I32)
        lval_sb = b.sb("lval_sb", [1, 4], F32)
        lb_sb = b.sb("lb_sb", [P, 1], F32)
        negc_sb = b.sb("negc_sb", [P, 32], F32)
        norm_sb = b.sb("norm_sb", [1, 64], F32)
        xb_sb = b.sb("xb", [P, D], F32)
        sq_sb = b.sb("sq_sb", [P, D], F32)
        stat_sb = b.sb("stat_sb", [P, 8], F32)
        lnT_sb = b.sb("lnT_sb", [P, 16, P], BF16)
        attn_stack = ExitStack()
        kpair_sb = attn_stack.enter_context(nc.sbuf_tensor("kpair_sb", [P, 4, NT_CTX * P], BF16))
        vaug_sb = attn_stack.enter_context(nc.sbuf_tensor("vaug_sb", [P, 8, NT_CTX, 130], BF16))
        wk_sb = attn_stack.enter_context(nc.sbuf_tensor("wk_sb", [P, 16, 512], BF16))
        wv_sb = attn_stack.enter_context(nc.sbuf_tensor("wv_sb", [P, 16, 512], BF16))

        ps_scf = [b.ps(f"ps_scf{i}", [P, 1536], F32) for i in range(2)]
        ps_sc = [ps_scf[i][:, 0:9 * P] for i in range(2)]
        ps_av = [ps_scf[i][:, 1152:1282] for i in range(2)]
        ps_midf = b.ps("ps_midf", [P, 512], F32)
        ps_tr2 = [ps_midf[:, 0:128], ps_midf[:, 128:256]]
        ps_proj = [ps_midf[:, 256:384], ps_midf[:, 384:512]]
        ps_big = b.ps("ps_big", [P, 512], F32)
        ps_nrm = ps_big[0:1, :]

        for dst, src in ((ident_sb, identity), (ident_f32, identity32),
                         (tri0_sb, tri0), (tri8_sb, tri8),
                         (cmask_sb, cmask_in), (iotaJ_sb, iotaJ), (iotaI_sb, iotaI),
                         (ones_sb, onescol), (onesrow_sb, onesrow), (cu_sb, cuq)):
            b.cS.inc(S.dma_start(out=dst[:], in_=src), 16)
        b.cG.inc(G.dma_start(out=wk_sb[:], in_=wkT.rearrange("(f p) c -> p f c", p=P)), 16)
        b.cG.inc(G.dma_start(out=wv_sb[:], in_=wvT.rearrange("(f p) c -> p f c", p=P)), 16)
        b.wait(V, b.cS)
        V.tensor_sub(cu_sb[0:1, 2:3], cu_sb[0:1, 1:2], cu_sb[0:1, 0:1])
        V.tensor_copy(lval_sb[0:1, 0:1], cu_sb[0:1, 2:3])
        b.wait(T, b.cV)
        b.cP.inc(T.matmul(ps_proj[0][:, 0:1], onesrow_sb[:], lval_sb[0:1, 0:1],
                          start=True, stop=True))
        b.wait(V, b.cP)
        V.tensor_copy(lb_sb[:], ps_proj[0][:, 0:1])
        V.tensor_tensor(aq_sb[:], iotaI_sb[:], lb_sb[:].to_broadcast([P, NT_OWN]), op=ALU.is_lt)
        V.tensor_tensor(bprime_sb[:], iotaJ_sb[:], lb_sb[:].to_broadcast([P, NT_CTX]), op=ALU.is_ge)
        V.tensor_mul(bprime_sb[:], bprime_sb[:], cmask_sb[:])
        V.tensor_copy(aqU_sb[:], aq_sb[:])
        V.memset(norm_sb[:], 0.0)

        for t in range(NT_CTX):
            b.sync_all(S)
            b.cS.inc(S.dma_start(out=xb_sb[:], in_=x0h[t * P:(t + 1) * P, :]), 16)
            b.wait(A, b.cS)
            emit_ln(b, xb_sb, sq_sb, stat_sb, sq_sb)
            b.wait(T, b.cA)
            for f in range(16):
                ptr = ps_tr2[f % 2]
                b.cP.inc(T.transpose(ptr, sq_sb[:, f * P:(f + 1) * P], ident_f32[:]))
                b.wait(V, b.cP)
                V.tensor_copy(lnT_sb[:, f, :], ptr)
                b.wait(T, b.cV)
            b.sync_all(S)
            b.cS.inc(S.dma_start(out=lnT_dram[:, :, t * P:(t + 1) * P].rearrange("f p q -> p f q"),
                                 in_=lnT_sb[:]), 16)
            b.wait(T, b.cG)
            for j in range(4):
                pp = ps_proj[j % 2]
                b.sync_all(T)
                for f in range(16):
                    b.cP.inc(T.matmul(pp[:], wk_sb[:, f, j * P:(j + 1) * P], lnT_sb[:, f, :],
                                      start=(f == 0), stop=(f == 15)))
                b.wait(V, b.cP)
                V.tensor_copy(kpair_sb[:, j, t * P:(t + 1) * P], pp[:])
            b.sync_all(T)
            for f in range(16):
                b.cP.inc(T.matmul(ps_big[:], lnT_sb[:, f, :], wv_sb[:, f, :],
                                  start=(f == 0), stop=(f == 15)))
            b.wait(V, b.cP)
            for h in range(8):
                vsl = ps_big[:, h * HD:(h + 1) * HD]
                V.tensor_scalar(vaug_sb[:, h, t, 0:64], vsl, cmask_sb[:, t:t + 1], None, op0=ALU.mult)
                V.tensor_scalar(vaug_sb[:, h, t, 65:129], vsl, bprime_sb[:, t:t + 1], None, op0=ALU.mult)
                V.tensor_copy(vaug_sb[:, h, t, 64:65], cmask_sb[:, t:t + 1])
                V.tensor_copy(vaug_sb[:, h, t, 129:130], bprime_sb[:, t:t + 1])
            V.engine_nop()
            b.wait(T, b.cV)

        for j in range(4):
            for blk in range(6):
                cols = slice(blk * 512, (blk + 1) * 512)
                b.sync_all(A)
                A.activation(out=sq_sb[:, 0:512], in_=kpair_sb[:, j, cols], func=AF.Square)
                b.wait(T, b.cA)
                b.cP.inc(T.matmul(ps_nrm[0:1, 0:512], ones_sb[0:64, :], sq_sb[0:64, 0:512],
                                  start=True, stop=True))
                b.wait(V, b.cP)
                V.reduce_max(stat_sb[0:1, 0:1], ps_nrm[0:1, 0:512], axis=AX.X)
                V.tensor_max(norm_sb[0:1, 32 + 2 * j:32 + 2 * j + 1],
                             norm_sb[0:1, 32 + 2 * j:32 + 2 * j + 1], stat_sb[0:1, 0:1])
                V.engine_nop()
                b.wait(T, b.cV)
                b.cP.inc(T.matmul(ps_nrm[0:1, 0:512], ones_sb[64:128, :], sq_sb[64:128, 0:512],
                                  start=True, stop=True))
                b.wait(V, b.cP)
                V.reduce_max(stat_sb[0:1, 0:1], ps_nrm[0:1, 0:512], axis=AX.X)
                V.tensor_max(norm_sb[0:1, 32 + 2 * j + 1:32 + 2 * j + 2],
                                      norm_sb[0:1, 32 + 2 * j + 1:32 + 2 * j + 2], stat_sb[0:1, 0:1])

        with ExitStack() as qctx:
            wq_sb = qctx.enter_context(nc.sbuf_tensor("wq_sb", [P, 16, 16, P], BF16))
            qall_sb = qctx.enter_context(nc.sbuf_tensor("qall_sb", [P, 16, P], BF16))
            qsq_sb = qctx.enter_context(nc.sbuf_tensor("qsq_sb", [P, P], F32))
            b.sync_all(G)
            b.cG.inc(G.dma_start(out=wq_sb[:],
                                 in_=wqT.rearrange("(f p) (m c) -> p f m c", p=P, c=P)), 16)
            for i in range(NT_OWN):
                t = 8 + i
                b.sync_all(S)
                b.cS.inc(S.dma_start(out=lnT_sb[:],
                                     in_=lnT_dram[:, :, t * P:(t + 1) * P].rearrange("f p q -> p f q")), 16)
                b.wait(T, b.cS, b.cG)
                for m in range(16):
                    pp = ps_proj[m % 2]
                    b.sync_all(T)
                    for f in range(16):
                        b.cP.inc(T.matmul(pp[:], wq_sb[:, f, m, :], lnT_sb[:, f, :],
                                          start=(f == 0), stop=(f == 15)))
                    b.wait(V, b.cP)
                    V.tensor_scalar_mul(qall_sb[:, m, :], pp[:], 0.125)
                    b.wait(A, b.cV)
                    A.activation(out=qsq_sb[:], in_=qall_sb[:, m, :], func=AF.Square)
                    b.wait(T, b.cA)
                    hl, hh = PAIR_HEADS[m]
                    b.cP.inc(T.matmul(ps_nrm[0:1, 0:P], ones_sb[0:64, :], qsq_sb[0:64, :],
                                      start=True, stop=True))
                    b.wait(V, b.cP)
                    V.reduce_max(stat_sb[0:1, 1:2], ps_nrm[0:1, 0:P], axis=AX.X)
                    V.tensor_max(norm_sb[0:1, hl:hl + 1], norm_sb[0:1, hl:hl + 1],
                                          stat_sb[0:1, 1:2])
                    b.wait(T, b.cV)
                    b.cP.inc(T.matmul(ps_nrm[0:1, 0:P], ones_sb[64:128, :], qsq_sb[64:128, :],
                                      start=True, stop=True))
                    b.wait(V, b.cP)
                    V.reduce_max(stat_sb[0:1, 1:2], ps_nrm[0:1, 0:P], axis=AX.X)
                    V.tensor_max(norm_sb[0:1, hh:hh + 1], norm_sb[0:1, hh:hh + 1],
                                          stat_sb[0:1, 1:2])
                b.sync_all(S)
                b.cS.inc(S.dma_start(out=q_dram[:, :, i * P:(i + 1) * P].rearrange("m p q -> p m q"),
                                     in_=qall_sb[:]), 16)

        b.sync_all(V)
        for h in range(32):
            V.tensor_mul(sq_sb[0:1, h:h + 1], norm_sb[0:1, h:h + 1],
                                  norm_sb[0:1, 32 + h // 4:33 + h // 4])
        b.wait(A, b.cV)
        A.activation(out=sq_sb[0:1, 32:64], in_=sq_sb[0:1, 0:32], func=AF.Ln)
        A.activation(out=sq_sb[0:1, 32:64], in_=sq_sb[0:1, 32:64], func=AF.Exp, scale=0.5)
        b.wait(V, b.cA)
        V.tensor_scalar_mul(sq_sb[0:1, 32:64], sq_sb[0:1, 32:64], -1.0)
        b.wait(T, b.cV)
        b.cP.inc(T.matmul(ps_big[:, 0:32], onesrow_sb[:], sq_sb[0:1, 32:64],
                          start=True, stop=True))
        b.wait(V, b.cP)
        V.tensor_copy(negc_sb[:], ps_big[:, 0:32])
        if DEBUG:
            b.sync_all(S)
            b.cS.inc(S.dma_start(out=dbg_norm, in_=norm_sb[:]), 16)
            b.cS.inc(S.dma_start(out=dbg_negc, in_=negc_sb[:]), 16)
            b.cS.inc(S.dma_start(out=dbg_kpair, in_=kpair_sb[:]), 16)
            b.cS.inc(S.dma_start(out=dbg_vaug, in_=vaug_sb[:]), 16)

        with ExitStack() as actx:
            qtile_sb = actx.enter_context(nc.sbuf_tensor("qtile_sb", [P, 16, P], BF16))
            probs_sb = [actx.enter_context(nc.sbuf_tensor(f"probs{i}", [P, 9 * P], BF16))
                        for i in range(2)]
            avf_sb = actx.enter_context(nc.sbuf_tensor("avf_sb", [P, 130], F32))
            avsel_sb = actx.enter_context(nc.sbuf_tensor("avsel_sb", [P, HD], F32))
            den_sb = actx.enter_context(nc.sbuf_tensor("den_sb", [P, 1], F32))
            atok_sb = actx.enter_context(nc.sbuf_tensor("atok_sb", [P, HD], F32))
            aT_sb = actx.enter_context(nc.sbuf_tensor("aT_sb", [P, 16, P], BF16))
            for qt in range(NT_OWN):
                b.sync_all(S)
                b.cS.inc(S.dma_start(out=qtile_sb[:],
                                     in_=q_dram[:, :, qt * P:(qt + 1) * P].rearrange("m p q -> p m q")), 16)
                for m in range(16):
                    j = m // 4
                    hl, hh = PAIR_HEADS[m]
                    kcol0 = qt * P
                    b.sync_all(T)
                    for kk in range(9):
                        ksl = slice(kcol0 + kk * P, kcol0 + (kk + 1) * P)
                        psl = slice(kk * P, (kk + 1) * P)
                        T.matmul(ps_sc[0][:, psl], kpair_sb[0:64, j, ksl], qtile_sb[0:64, m, :],
                                 start=True, stop=True, tile_position=(0, 0))
                        b.cP.inc(T.matmul(ps_sc[1][:, psl], kpair_sb[64:128, j, ksl],
                                          qtile_sb[64:128, m, :], start=True, stop=True,
                                          tile_position=(64, 0)))
                    b.wait(A, b.cP)
                    for i, h in enumerate((hl, hh)):
                        A.activation(out=probs_sb[i][:], in_=ps_sc[i][:],
                                              func=AF.Exp, bias=negc_sb[:, h:h + 1], scale=1.0)
                    b.wait(V, b.cA)
                    for i in range(2):
                        V.tensor_mul(probs_sb[i][:, 0:P], probs_sb[i][:, 0:P], tri0_sb[:])
                        V.tensor_mul(probs_sb[i][:, 8 * P:9 * P], probs_sb[i][:, 8 * P:9 * P], tri8_sb[:])
                    V.engine_nop()
                    if DEBUG and qt == 4 and m == 0:
                        b.sync_all(S)
                        for i in range(2):
                            b.cS.inc(S.dma_start(out=dbg_probs[i], in_=probs_sb[i][:]), 16)
                    b.wait(T, b.cV)
                    for i, h in enumerate((hl, hh)):
                        kvh = h // 4
                        for kk in range(9):
                            kt_g = qt + kk
                            b.cP.inc(T.matmul(ps_av[i][:], probs_sb[i][:, kk * P:(kk + 1) * P],
                                              vaug_sb[:, kvh, kt_g, :],
                                              start=(kk == 0), stop=(kk == 8)))
                    b.wait(V, b.cP)
                    for i, h in enumerate((hl, hh)):
                        V.tensor_copy(avf_sb[:], ps_av[i][:])
                        V.engine_nop()
                        V.select(den_sb[:], aqU_sb[:, qt:qt + 1], avf_sb[:, 64:65], avf_sb[:, 129:130])
                        V.select(avsel_sb[:], aqU_sb[:, qt:qt + 1].to_broadcast([P, HD]),
                                 avf_sb[:, 0:64], avf_sb[:, 65:129])
                        V.reciprocal(den_sb[:], den_sb[:])
                        b.wait(V, b.cV)
                        V.tensor_mul(atok_sb[:], avsel_sb[:],
                                              den_sb[:].to_broadcast([P, HD]))
                        b.wait(T, b.cV)
                        b.cP.inc(T.transpose(ps_tr2[i][0:64, :], atok_sb[:], ident_f32[:]))
                        b.wait(V, b.cP)
                        base = 0 if i == 0 else 64
                        V.tensor_copy(aT_sb[base:base + 64, m, :], ps_tr2[i][0:64, :])
                        b.wait(T, b.cV)
                b.sync_all(S)
                b.cS.inc(S.dma_start(out=aT_dram[:, :, qt * P:(qt + 1) * P].rearrange("m p q -> p m q"),
                                     in_=aT_sb[:]), 16)

        attn_stack.close()
        with ExitStack() as octx:
            wo_sb = octx.enter_context(nc.sbuf_tensor("wo_sb", [P, 16, 512], BF16))
            aTt_sb = octx.enter_context(nc.sbuf_tensor("aTt_sb", [P, 16, P], BF16))
            xo_sb = octx.enter_context(nc.sbuf_tensor("xo_sb", [P, 512], F32))
            x1_sb = octx.enter_context(nc.sbuf_tensor("x1_sb", [P, 512], F32))
            for cch in range(4):
                ccols = slice(cch * 512, (cch + 1) * 512)
                b.sync_all(G)
                b.cG.inc(G.dma_start(out=wo_sb[:],
                                     in_=woT[:, ccols].rearrange("(f p) c -> p f c", p=P)), 16)
                for qt in range(NT_OWN):
                    b.sync_all(S)
                    b.cS.inc(S.dma_start(out=aTt_sb[:],
                                         in_=aT_dram[:, :, qt * P:(qt + 1) * P].rearrange("m p q -> p m q")), 16)
                    b.cS.inc(S.dma_start(out=xo_sb[:],
                                         in_=x0h[(8 + qt) * P:(9 + qt) * P, ccols]), 16)
                    b.wait(T, b.cS, b.cG)
                    for m in range(16):
                        b.cP.inc(T.matmul(ps_big[:], aTt_sb[:, m, :], wo_sb[:, m, :],
                                          start=(m == 0), stop=(m == 15)))
                    b.wait(V, b.cP)
                    V.tensor_add(x1_sb[:], xo_sb[:], ps_big[:])
                    b.sync_all(S)
                    b.cS.inc(S.dma_start(out=x1_out[qt * P:(qt + 1) * P, ccols], in_=x1_sb[:]), 16)
        with ExitStack() as fctx:
            fe = fctx.enter_context
            wqf_sb = fe(nc.sbuf_tensor("wqf_sb", [P, 16, 512], BF16))
            keys_sb = fe(nc.sbuf_tensor("keys_sb", [P, 8, P], BF16))
            iota64_sb = fe(nc.sbuf_tensor("iota64_sb", [P, 64], F32))
            lnf_sb = fe(nc.sbuf_tensor("lnf_sb", [P, D], F32))
            qf_sb = fe(nc.sbuf_tensor("qf_sb", [P, 4, P], BF16))
            sims_sb = fe(nc.sbuf_tensor("sims_sb", [P, 8, P], F32))
            t8_sb = fe(nc.sbuf_tensor("t8_sb", [P, 8, 8], F32))
            i8u_sb = fe(nc.sbuf_tensor("i8u_sb", [P, 8, 8], U32))
            i8f_sb = fe(nc.sbuf_tensor("i8f_sb", [P, 8, 8], F32))
            cand_sb = fe(nc.sbuf_tensor("cand_sb", [P, 64], F32))
            emat_sb = fe(nc.sbuf_tensor("emat_sb", [P, 64], F32))
            f8_sb = fe(nc.sbuf_tensor("f8_sb", [P, 4, 8], F32))
            c8u_sb = fe(nc.sbuf_tensor("c8u_sb", [P, 8], U32))
            c8f_sb = fe(nc.sbuf_tensor("c8f_sb", [P, 8], F32))
            oh_sb = fe(nc.sbuf_tensor("oh_sb", [P, 64], F32))
            esel_sb = fe(nc.sbuf_tensor("esel_sb", [P, 32], F32))
            eidx_sb = fe(nc.sbuf_tensor("eidx_sb", [P, 32], U32))
            fsr_sb = fe(nc.sbuf_tensor("fsr_sb", [P, 4, 8], F32))
            hid_sb = fe(nc.sbuf_tensor("hid_sb", [P, 32], F32))
            coeff_sb = fe(nc.sbuf_tensor("coeff_sb", [P, 32], F32))
            gdn_sb = fe(nc.sbuf_tensor("gdn_sb", [P, D], F32))
            gup_sb = fe(nc.sbuf_tensor("gup_sb", [P, D], F32))
            acc_sb = fe(nc.sbuf_tensor("acc_sb", [P, D], F32))
            junk_sb = fe(nc.sbuf_tensor("junk_sb", [P, 72], F32))
            junk2_sb = fe(nc.sbuf_tensor("junk2_sb", [P, D], F32))
            b.sync_all(G)
            b.cG.inc(G.dma_start(out=wqf_sb[:], in_=wqf.rearrange("(f p) c -> p f c", p=P)), 16)
            b.cG.inc(G.dma_start(out=keys_sb[0:64, :, :], in_=keysT), 16)
            b.cG.inc(G.dma_start(out=keys_sb[64:128, :, :], in_=keysT), 16)
            b.cS.inc(S.dma_start(out=iota64_sb[:], in_=iota64), 16)
            b.wait(T, b.cG)
            b.wait(V, b.cS)
            env = dict(dbg=dbg_f if DEBUG else None,
                       ps_sims=[ps_scf[0][:, 0:P], ps_scf[1][:, 0:P]],
                       T=T, V=V, A=A, S=S, G=G, ident_f32=ident_f32, lnT_sb=lnT_sb,
                       ps_proj=ps_proj, ps_tr2=ps_tr2, e_down=e_down, e_up=e_up,
                       junk2=junk2_sb,
                       ffn_bufs=(wqf_sb, keys_sb, iota64_sb, lnf_sb, None, stat_sb,
                                 xb_sb, qf_sb, sims_sb, t8_sb, i8u_sb, i8f_sb,
                                 cand_sb, emat_sb, f8_sb, c8u_sb, c8f_sb, oh_sb,
                                 esel_sb, eidx_sb, fsr_sb, hid_sb, coeff_sb,
                                 gdn_sb, gup_sb, acc_sb, junk_sb))
            emit_ffn(b, env, x1_out, x2_out, 0)
        with ExitStack() as dctx:
            de = dctx.enter_context
            k3_sb = de(nc.sbuf_tensor("k3_sb", [P, 4, D], BF16))
            v3_sb = de(nc.sbuf_tensor("v3_sb", [P, 16, 512], BF16))
            memq_sb = de(nc.sbuf_tensor("memq_sb", [P, 16, 6, 129], BF16))
            dpfp_sb = de(nc.sbuf_tensor("dpfp_sb", [P, 6, D], BF16))
            dptok_sb = de(nc.sbuf_tensor("dptok_sb", [P, 16, 768], BF16))
            xp_sb = de(nc.sbuf_tensor("xp_sb", [P, D], BF16))
            xn_sb = de(nc.sbuf_tensor("xn_sb", [P, D], BF16))
            rolled_sb = de(nc.sbuf_tensor("rolled_sb", [P, 2, D], BF16))
            wkv3_sb = de(nc.sbuf_tensor("wkv3_sb", [P, 16, 1024], BF16))
            bkv3_sb = de(nc.sbuf_tensor("bkv3_sb", [1, 1024], BF16))
            wmbc_sb = de(nc.sbuf_tensor("wmbc_sb", [P, 16, 512], BF16))
            bmb_sb = de(nc.sbuf_tensor("bmb_sb", [1, D], BF16))
            onesrow_bf = de(nc.sbuf_tensor("onesrow_bf", [1, P], BF16))
            mbst_sb = de(nc.sbuf_tensor("mbst_sb", [P, 512], BF16))
            mbh_sb = de(nc.sbuf_tensor("mbh_sb", [P, 16, P], BF16))
            wmv_sb = de(nc.sbuf_tensor("wmv_sb", [P, 16, P], BF16))
            coefs_sb = de(nc.sbuf_tensor("coefs_sb", [P, 16, 4], BF16))
            snorm_sb = de(nc.sbuf_tensor("snorm_sb", [P, 16], F32))
            rnorm_sb = de(nc.sbuf_tensor("rnorm_sb", [P, 16], F32))
            tchunk_sb = sq_sb[:, 0:P]
            pv_sb = sq_sb[:, P:2 * P]
            evac_sb = sq_sb[:, 2 * P:3 * P]
            dn1_sb = sq_sb[:, 3 * P:3 * P + 8]
            if DEBUG:
                pvx_sb = xp_sb.rearrange("p (a c) -> p a c", c=P)
                numx_sb = xn_sb.rearrange("p (a c) -> p a c", c=P)
                denx_sb = sq_sb[:, 1792:1824].rearrange("p (a c) -> p a c", c=2)
            ps_nd = ps_scf[0][:, 0:129]
            ps_dl = ps_scf[1][:, 0:P]
            ps_dn = ps_scf[1][:, 256:260]

            b.sync_all(G)
            b.cG.inc(G.dma_start(out=wkv3_sb[:, :, 0:512], in_=wk3T.rearrange("(f p) c -> p f c", p=P)), 16)
            b.cG.inc(G.dma_start(out=wkv3_sb[:, :, 512:1024], in_=wv3T.rearrange("(f p) c -> p f c", p=P)), 16)
            b.cG.inc(G.dma_start(out=bkv3_sb[0:1, 0:512], in_=bk3), 16)
            b.cG.inc(G.dma_start(out=bkv3_sb[0:1, 512:1024], in_=bv3), 16)
            b.cG.inc(G.dma_start(out=bmb_sb[:], in_=bmb3), 16)
            b.cG.inc(G.dma_start(out=memq_sb[:], in_=meml.rearrange("h f p c -> p h f c")), 16)
            V.tensor_copy(onesrow_bf[:], onesrow_sb[:])

            for i in range(NT_OWN):
                b.sync_all(S)
                b.cS.inc(S.dma_start(out=xb_sb[:], in_=x2_out[i * P:(i + 1) * P, :]), 16)
                b.wait(A, b.cS)
                b.wait(V, b.cS)
                emit_ln(b, xb_sb, sq_sb, stat_sb, sq_sb)
                b.wait(T, b.cA)
                for f in range(16):
                    b.cP.inc(T.transpose(ps_tr2[f % 2], sq_sb[:, f * P:(f + 1) * P], ident_f32[:]))
                    b.wait(V, b.cP)
                    V.tensor_copy(lnT_sb[:, f, :], ps_tr2[f % 2])
                    b.wait(T, b.cV)
                b.wait(T, b.cG)
                for cc in range(4):
                    pp = ps_proj[cc % 2]
                    b.sync_all(T)
                    for f in range(16):
                        T.matmul(pp, wkv3_sb[:, f, cc * P:(cc + 1) * P], lnT_sb[:, f, :],
                                 start=(f == 0), stop=False)
                    b.cP.inc(T.matmul(pp, bkv3_sb[0:1, cc * P:(cc + 1) * P],
                                      onesrow_bf[:], start=False, stop=True))
                    b.wait(V, b.cP)
                    V.tensor_copy(k3_sb[:, cc, i * P:(i + 1) * P], pp)
                    b.wait(T, b.cV)
                b.sync_all(T)
                for f in range(16):
                    T.matmul(ps_big[:, 0:512], lnT_sb[:, f, :], wkv3_sb[:, f, 512:1024],
                             start=(f == 0), stop=False)
                b.cP.inc(T.matmul(ps_big[:, 0:512], onesrow_bf[:], bkv3_sb[0:1, 512:1024],
                                  start=False, stop=True))
                b.wait(V, b.cP)
                V.tensor_copy(v3_sb[:, i, :], ps_big[:, 0:512])
                b.wait(T, b.cV)
                for cc in range(4):
                    b.sync_all(T)
                    b.sync_all(G)
                    b.cG.inc(G.dma_start(out=wmbc_sb[:],
                                         in_=wmb3T[:, cc * 512:(cc + 1) * 512]
                                         .rearrange("(f p) c -> p f c", p=P)), 16)
                    b.wait(T, b.cG)
                    for f in range(16):
                        T.matmul(ps_big[:, 0:512], lnT_sb[:, f, :], wmbc_sb[:, f, :],
                                 start=(f == 0), stop=False)
                    b.cP.inc(T.matmul(ps_big[:, 0:512], onesrow_bf[:],
                                      bmb_sb[0:1, cc * 512:(cc + 1) * 512],
                                      start=False, stop=True))
                    b.wait(A, b.cP)
                    b.wait(A, b.cS)
                    A.activation(out=mbst_sb[:], in_=ps_big[:, 0:512], func=AF.Sigmoid)
                    b.sync_all(S)
                    b.cS.inc(S.dma_start(out=mb3_dram[i * P:(i + 1) * P, cc * 512:(cc + 1) * 512],
                                         in_=mbst_sb[:]), 16)
            for hp in range(4):
                b.sync_all(V)
                V.memset(snorm_sb[:], 0.0)
                V.tensor_scalar_max(xp_sb[:], k3_sb[:, hp, :], 0.0)
                V.tensor_scalar(xn_sb[:], k3_sb[:, hp, :], -1.0, 0.0, op0=ALU.mult, op1=ALU.max)
                for j in (1, 2, 3):
                    b.sync_all(S)
                    b.cS.inc(S.dma_start(out=rolled_sb[0:j, 0, :], in_=xn_sb[P - j:P, :]), 16)
                    b.cS.inc(S.dma_start(out=rolled_sb[j:P, 0, :], in_=xp_sb[0:P - j, :]), 16)
                    b.cS.inc(S.dma_start(out=rolled_sb[0:j, 1, :], in_=xp_sb[P - j:P, :]), 16)
                    b.cS.inc(S.dma_start(out=rolled_sb[j:P, 1, :], in_=xn_sb[0:P - j, :]), 16)
                    b.wait(V, b.cS)
                    V.tensor_mul(dpfp_sb[:, 2 * (j - 1), :], xp_sb[:], rolled_sb[:, 0, :])
                    V.tensor_mul(dpfp_sb[:, 2 * (j - 1) + 1, :], xn_sb[:], rolled_sb[:, 1, :])
                for tt in range(NT_OWN):
                    for fc in range(6):
                        V.tensor_copy(tchunk_sb, dpfp_sb[:, fc, tt * P:(tt + 1) * P])
                        b.wait(T, b.cV)
                        b.cP.inc(T.transpose(ps_tr2[fc % 2], tchunk_sb, ident_f32[:]))
                        b.wait(V, b.cP)
                        V.tensor_copy(dptok_sb[:, tt, fc * P:(fc + 1) * P], ps_tr2[fc % 2])
                        b.wait(T, b.cV)
                    b.wait(A, b.cV)
                    A.activation(out=sq_sb[:, 1024:1792], in_=dptok_sb[:, tt, :], func=AF.Square,
                                 accum_out=snorm_sb[:, tt:tt + 1])
                b.wait(A, b.cA)
                A.activation(out=rnorm_sb[:], in_=snorm_sb[:], func=AF.Ln)
                A.activation(out=snorm_sb[:], in_=rnorm_sb[:], func=AF.Exp, scale=0.5)
                b.wait(V, b.cA)
                V.tensor_scalar_max(snorm_sb[:], snorm_sb[:], 1e-12)
                V.tensor_copy(rnorm_sb[:], snorm_sb[:])
                V.reciprocal(rnorm_sb[:], rnorm_sb[:])
                for r in range(4):
                    h = 4 * hp + r
                    b.sync_all(S)
                    b.cS.inc(S.dma_start(out=mbh_sb[:],
                                         in_=mb3_dram[:, h * P:(h + 1) * P]
                                         .rearrange("(a p) c -> p a c", p=P)), 16)
                    b.wait(V, b.cS)
                    for tt in range(NT_OWN):
                        b.sync_all(T)
                        for fc in range(6):
                            mmi = T.matmul(ps_nd, dpfp_sb[:, fc, tt * P:(tt + 1) * P],
                                           memq_sb[:, h, fc, :], start=(fc == 0), stop=(fc == 5))
                        b.cP.inc(mmi)
                        b.wait(V, b.cP)
                        V.scalar_tensor_tensor(out=stat_sb[:, 0:1], in0=snorm_sb[:, tt:tt + 1],
                                               scalar=1e-8, in1=ps_nd[:, 128:129],
                                               op0=ALU.mult, op1=ALU.add)
                        V.reciprocal(stat_sb[:, 1:2], stat_sb[:, 0:1])
                        V.tensor_scalar(pv_sb, ps_nd[:, 0:P], stat_sb[:, 1:2], None, op0=ALU.mult)
                        if DEBUG and hp == 0 and r == 3:
                            V.tensor_copy(pvx_sb[:, tt, :], pv_sb)
                            V.tensor_copy(numx_sb[:, tt, :], ps_nd[:, 0:P])
                            V.tensor_copy(denx_sb[:, tt, 0:1], ps_nd[:, 128:129])
                            V.tensor_copy(denx_sb[:, tt, 1:2], stat_sb[:, 0:1])
                        V.scalar_tensor_tensor(out=stat_sb[:, 2:3], in0=ps_nd[:, 128:129],
                                               scalar=1.0, in1=rnorm_sb[:, tt:tt + 1],
                                               op0=ALU.mult, op1=ALU.mult)
                        V.tensor_scalar(stat_sb[:, 2:3], stat_sb[:, 2:3], -1.0, 1.0 - 1e-8,
                                        op0=ALU.mult, op1=ALU.add)
                        V.tensor_scalar_max(stat_sb[:, 2:3], stat_sb[:, 2:3], 0.0)
                        V.tensor_scalar_min(stat_sb[:, 2:3], stat_sb[:, 2:3], 1.0)
                        V.tensor_mul(coefs_sb[:, tt, r:r + 1], stat_sb[:, 2:3], rnorm_sb[:, tt:tt + 1])
                        V.tensor_sub(pv_sb, v3_sb[:, tt, hp * P:(hp + 1) * P], pv_sb)
                        V.tensor_scalar(pv_sb, pv_sb, rnorm_sb[:, tt:tt + 1], None, op0=ALU.mult)
                        V.tensor_mul(wmv_sb[:, tt, :], pv_sb, mbh_sb[:, tt, :])
                    for fc in range(6):
                        b.sync_all(T)
                        for tt in range(NT_OWN):
                            mmi = T.matmul(ps_dl, dptok_sb[:, tt, fc * P:(fc + 1) * P],
                                           wmv_sb[:, tt, :], start=(tt == 0), stop=(tt == 15))
                        b.cP.inc(mmi)
                        b.wait(V, b.cP)
                        b.wait(V, b.cS)
                        V.tensor_copy(evac_sb, ps_dl)
                        b.sync_all(S)
                        b.cS.inc(S.dma_start(out=delta_out[h, fc], in_=evac_sb), 16)
                if DEBUG and hp == 0:
                    d = dbg_b
                    b.sync_all(S)
                    b.cS.inc(S.dma_start(out=d["pvall"], in_=pvx_sb[:]), 16)
                    b.cS.inc(S.dma_start(out=d["mbh"], in_=mbh_sb[:]), 16)
                    b.cS.inc(S.dma_start(out=d["numall"], in_=numx_sb[:]), 16)
                    b.cS.inc(S.dma_start(out=d["denall"], in_=denx_sb[:]), 16)
                    b.cS.inc(S.dma_start(out=d["dptok"], in_=dptok_sb[:]), 16)
                    b.cS.inc(S.dma_start(out=d["snorm"], in_=snorm_sb[:]), 16)
                    b.cS.inc(S.dma_start(out=d["rnorm"], in_=rnorm_sb[:]), 16)
                    b.cS.inc(S.dma_start(out=d["wmv"], in_=wmv_sb[:]), 16)
                    b.cS.inc(S.dma_start(out=d["coefs"], in_=coefs_sb[:]), 16)
                    b.cS.inc(S.dma_start(out=d["dpfp"], in_=dpfp_sb[:]), 16)
                    b.cS.inc(S.dma_start(out=d["k3"], in_=k3_sb[:]), 16)
                    b.cS.inc(S.dma_start(out=d["v3"], in_=v3_sb[:]), 16)
                b.wait(V, b.cV)
                for fc in range(6):
                    b.sync_all(T)
                    for tt in range(NT_OWN):
                        mmi = T.matmul(ps_dn, dptok_sb[:, tt, fc * P:(fc + 1) * P],
                                       coefs_sb[:, tt, :], start=(tt == 0), stop=(tt == 15))
                    b.cP.inc(mmi)
                    b.wait(V, b.cP)
                    b.wait(V, b.cS)
                    V.tensor_copy(dn1_sb[:, 0:4], ps_dn)
                    b.sync_all(S)
                    b.cS.inc(S.dma_start(out=dnorm_out[fc, :, 4 * hp:4 * hp + 4],
                                         in_=dn1_sb[:, 0:4]), 16)
        b.sync_all(S)
        b.sync_all(G)
    return nc


def build_l2():
    nc = bass.Bass("TRN2", target_bir_lowering=False)
    dt = nc.dram_tensor
    x2in = dt("x2in", [NT_OWN * P, D], F32, kind="ExternalInput").ap()
    meml = dt("meml", [16, 6, P, 129], F32, kind="ExternalInput").ap()
    delta_all = dt("delta_all", [16, 6, P, 8, P], F32, kind="ExternalInput").ap()
    dnorm_all = dt("dnorm_all", [6, P, 16, 8], F32, kind="ExternalInput").ap()
    wq3T = dt("wq3T", [D, D], F32, kind="ExternalInput").ap()
    bq3 = dt("bq3", [1, D], F32, kind="ExternalInput").ap()
    wg3T = dt("wg3T", [D, D], F32, kind="ExternalInput").ap()
    bg3 = dt("bg3", [1, D], F32, kind="ExternalInput").ap()
    wqf = dt("wqf", [D, 512], F32, kind="ExternalInput").ap()
    keysT = dt("keysT", [64, 8, P], F32, kind="ExternalInput").ap()
    e_down = dt("e_down", [16384, D], F32, kind="ExternalInput").ap()
    e_up = dt("e_up", [16384, D], F32, kind="ExternalInput").ap()
    iota64 = dt("iota64", [P, 64], F32, kind="ExternalInput").ap()
    identity32 = dt("identity32", [P, P], F32, kind="ExternalInput").ap()
    onescol = dt("onescol", [P, 1], F32, kind="ExternalInput").ap()
    onesrow = dt("onesrow", [1, P], F32, kind="ExternalInput").ap()
    y_out = dt("y_out", [NT_OWN * P, D], F32, kind="ExternalOutput").ap()
    dk = "ExternalOutput" if DEBUG else "Internal"
    lnf_dram = dt("lnf_dram", [NT_OWN * P, D], F32, kind=dk).ap()
    lnT3_dram = dt("lnT3_dram", [16, P, NT_OWN * P], BF16, kind=dk).ap()
    q3_dram = dt("q3_dram", [16, P, NT_OWN * P], BF16, kind=dk).ap()
    mh_dram = dt("mh_dram", [NT_OWN * P, D], BF16, kind=dk).ap()
    x3_dram = dt("x3_dram", [NT_OWN * P, D], F32, kind=dk).ap()
    if DEBUG:
        dbg_memq = dt("dbg_memq", [P, 16, 6, 129], BF16, kind="ExternalOutput").ap()
        dbg_gate = dt("dbg_gate", [NT_OWN * P, D], BF16, kind="ExternalOutput").ap()

    with ExitStack() as ctx:
        b = B(nc, ctx)
        T, V, A, S, G = nc.tensor, b.V, b.A, nc.sync, nc.gpsimd

        ident_f32 = b.sb("ident_f32", [P, P], F32)
        ones_sb = b.sb("ones_sb", [P, 1], F32)
        onesbf_sb = b.sb("onesbf_sb", [P, 1], BF16)
        onesrow_sb = b.sb("onesrow_sb", [1, P], F32)
        onesrow_bf = b.sb("onesrow_bf", [1, P], BF16)
        xb_sb = b.sb("xb", [P, D], F32)
        sq_sb = b.sb("sq_sb", [P, D], F32)
        stat_sb = b.sb("stat_sb", [P, 8], F32)
        lnT_sb = b.sb("lnT_sb", [P, 16, P], BF16)
        memq_sb = b.sb("memq_sb", [P, 16, 6, 129], BF16)

        ps_scf = [b.ps(f"ps_scf{i}", [P, 1536], F32) for i in range(2)]
        ps_midf = b.ps("ps_midf", [P, 512], F32)
        ps_tr2 = [ps_midf[:, 0:128], ps_midf[:, 128:256]]
        ps_proj = [ps_midf[:, 256:384], ps_midf[:, 384:512]]
        ps_big = b.ps("ps_big", [P, 512], F32)
        ps_nd = ps_scf[0][:, 0:129]
        ps_nrm1 = ps_scf[1][:, 0:1]

        for dst, srcap in ((ident_f32, identity32), (ones_sb, onescol),
                           (onesrow_sb, onesrow)):
            b.cS.inc(S.dma_start(out=dst[:], in_=srcap), 16)
        b.wait(V, b.cS)
        V.tensor_copy(onesbf_sb[:], ones_sb[:])
        V.tensor_copy(onesrow_bf[:], onesrow_sb[:])

        with ExitStack() as ectx:
            ee = ectx.enter_context
            mpc_sb = ee(nc.sbuf_tensor("mpc_sb", [P, 129], F32))
            dpc_sb = ee(nc.sbuf_tensor("dpc_sb", [P, 8, P], F32))
            npc_sb = ee(nc.sbuf_tensor("npc_sb", [P, 8], F32))
            for h in range(16):
                for fc in range(6):
                    b.sync_all(S)
                    b.cS.inc(S.dma_start(out=mpc_sb[:], in_=meml[h, fc]), 16)
                    b.cS.inc(S.dma_start(out=dpc_sb[:], in_=delta_all[h, fc].rearrange("p c q -> p c q")), 16)
                    b.cS.inc(S.dma_start(out=npc_sb[:], in_=dnorm_all[fc, :, h, :]), 16)
                    b.wait(V, b.cS)
                    for c in range(8):
                        V.tensor_add(mpc_sb[:, 0:P], mpc_sb[:, 0:P], dpc_sb[:, c, :])
                        V.tensor_add(mpc_sb[:, 128:129], mpc_sb[:, 128:129], npc_sb[:, c:c + 1])
                    V.tensor_copy(memq_sb[:, h, fc, :], mpc_sb[:])
            if DEBUG:
                b.sync_all(S)
                b.cS.inc(S.dma_start(out=dbg_memq, in_=memq_sb[:]), 16)

        with ExitStack() as qctx:
            qe = qctx.enter_context
            wq_sb = qe(nc.sbuf_tensor("wq_sb", [P, 16, 16, P], BF16))
            bq_sb = qe(nc.sbuf_tensor("bq_sb", [1, D], BF16))
            qall_sb = qe(nc.sbuf_tensor("qall_sb", [P, 16, P], BF16))
            b.sync_all(G)
            b.cG.inc(G.dma_start(out=wq_sb[:],
                                 in_=wq3T.rearrange("(f p) (m c) -> p f m c", p=P, c=P)), 16)
            b.cG.inc(G.dma_start(out=bq_sb[:], in_=bq3), 16)
            for i in range(NT_OWN):
                b.sync_all(S)
                b.cS.inc(S.dma_start(out=xb_sb[:], in_=x2in[i * P:(i + 1) * P, :]), 16)
                b.wait(A, b.cS)
                b.wait(V, b.cS)
                emit_ln(b, xb_sb, sq_sb, stat_sb, sq_sb)
                b.sync_all(S)
                b.cS.inc(S.dma_start(out=lnf_dram[i * P:(i + 1) * P, :], in_=sq_sb[:]), 16)
                b.wait(T, b.cA)
                for f in range(16):
                    b.cP.inc(T.transpose(ps_tr2[f % 2], sq_sb[:, f * P:(f + 1) * P], ident_f32[:]))
                    b.wait(V, b.cP)
                    V.tensor_copy(lnT_sb[:, f, :], ps_tr2[f % 2])
                    b.wait(T, b.cV)
                b.wait(S, b.cV)
                b.cS.inc(S.dma_start(out=lnT3_dram[:, :, i * P:(i + 1) * P].rearrange("f p q -> p f q"),
                                     in_=lnT_sb[:]), 16)
                b.wait(T, b.cG)
                for m in range(16):
                    pp = ps_proj[m % 2]
                    b.sync_all(T)
                    for f in range(16):
                        T.matmul(pp, wq_sb[:, f, m, :], lnT_sb[:, f, :],
                                 start=(f == 0), stop=False)
                    b.cP.inc(T.matmul(pp, bq_sb[0:1, m * P:(m + 1) * P], onesrow_bf[:],
                                      start=False, stop=True))
                    b.wait(V, b.cP)
                    V.tensor_copy(qall_sb[:, m, :], pp)
                    b.wait(T, b.cV)
                b.sync_all(S)
                b.cS.inc(S.dma_start(out=q3_dram[:, :, i * P:(i + 1) * P].rearrange("m p q -> p m q"),
                                     in_=qall_sb[:]), 16)

        with ExitStack() as hctx:
            he = hctx.enter_context
            qh_sb = he(nc.sbuf_tensor("qh_sb", [P, D], BF16))
            dpfp_sb = he(nc.sbuf_tensor("dpfp_sb", [P, 6, D], BF16))
            xp_sb = he(nc.sbuf_tensor("xp_sb", [P, D], BF16))
            xn_sb = he(nc.sbuf_tensor("xn_sb", [P, D], BF16))
            rolled_sb = he(nc.sbuf_tensor("rolled_sb", [P, 2, D], BF16))
            sqb_sb = he(nc.sbuf_tensor("sqb_sb", [P, D], BF16))
            snorm_sb = he(nc.sbuf_tensor("snorm_sb", [P, 16], F32))
            mhh_sb = he(nc.sbuf_tensor("mhh_sb", [P, 16, P], BF16))
            for h in range(16):
                b.sync_all(S)
                b.cS.inc(S.dma_start(out=qh_sb[:], in_=q3_dram[h]), 16)
                b.sync_all(V)
                b.sync_all(T)
                V.memset(snorm_sb[:], 0.0)
                V.tensor_scalar_max(xp_sb[:], qh_sb[:], 0.0)
                V.tensor_scalar(xn_sb[:], qh_sb[:], -1.0, 0.0, op0=ALU.mult, op1=ALU.max)
                for j in (1, 2, 3):
                    b.sync_all(S)
                    b.cS.inc(S.dma_start(out=rolled_sb[0:j, 0, :], in_=xn_sb[P - j:P, :]), 16)
                    b.cS.inc(S.dma_start(out=rolled_sb[j:P, 0, :], in_=xp_sb[0:P - j, :]), 16)
                    b.cS.inc(S.dma_start(out=rolled_sb[0:j, 1, :], in_=xp_sb[P - j:P, :]), 16)
                    b.cS.inc(S.dma_start(out=rolled_sb[j:P, 1, :], in_=xn_sb[0:P - j, :]), 16)
                    b.wait(V, b.cS)
                    V.tensor_mul(dpfp_sb[:, 2 * (j - 1), :], xp_sb[:], rolled_sb[:, 0, :])
                    V.tensor_mul(dpfp_sb[:, 2 * (j - 1) + 1, :], xn_sb[:], rolled_sb[:, 1, :])
                for fc in range(6):
                    b.wait(A, b.cV)
                    A.activation(out=sqb_sb[:], in_=dpfp_sb[:, fc, :], func=AF.Square)
                    b.wait(T, b.cA)
                    for tt in range(NT_OWN):
                        b.cP.inc(T.matmul(ps_scf[1][:, tt:tt + 1], sqb_sb[:, tt * P:(tt + 1) * P],
                                          onesbf_sb[:], start=(fc == 0), stop=(fc == 5)))
                b.wait(V, b.cP)
                V.tensor_copy(snorm_sb[:], ps_scf[1][:, 0:16])
                b.wait(A, b.cV)
                A.activation(out=sq_sb[:, 0:16], in_=snorm_sb[:], func=AF.Ln)
                A.activation(out=snorm_sb[:], in_=sq_sb[:, 0:16], func=AF.Exp, scale=0.5)
                b.wait(V, b.cA)
                V.tensor_scalar_max(snorm_sb[:], snorm_sb[:], 1e-12)
                for tt in range(NT_OWN):
                    b.sync_all(T)
                    for fc in range(6):
                        mmi = T.matmul(ps_nd, dpfp_sb[:, fc, tt * P:(tt + 1) * P],
                                       memq_sb[:, h, fc, :], start=(fc == 0), stop=(fc == 5))
                    b.cP.inc(mmi)
                    b.wait(V, b.cP)
                    V.scalar_tensor_tensor(out=stat_sb[:, 0:1], in0=snorm_sb[:, tt:tt + 1],
                                           scalar=1e-8, in1=ps_nd[:, 128:129],
                                           op0=ALU.mult, op1=ALU.add)
                    V.reciprocal(stat_sb[:, 1:2], stat_sb[:, 0:1])
                    V.tensor_scalar(mhh_sb[:, tt, :], ps_nd[:, 0:P], stat_sb[:, 1:2],
                                    None, op0=ALU.mult)
                b.sync_all(S)
                b.cS.inc(S.dma_start(out=mh_dram[:, h * P:(h + 1) * P]
                                     .rearrange("(a p) c -> p a c", p=P), in_=mhh_sb[:]), 16)

        with ExitStack() as gctx:
            ge = gctx.enter_context
            wg_sb = ge(nc.sbuf_tensor("wg_sb", [P, 16, 512], BF16))
            bg_sb = ge(nc.sbuf_tensor("bg_sb", [1, D], BF16))
            gst_sb = ge(nc.sbuf_tensor("gst_sb", [P, 512], BF16))
            mht_sb = ge(nc.sbuf_tensor("mht_sb", [P, D], BF16))
            lnx_sb = ge(nc.sbuf_tensor("lnx_sb", [P, D], F32))
            gate_sb = ge(nc.sbuf_tensor("gate_sb", [P, D], BF16))
            b.sync_all(G)
            b.cG.inc(G.dma_start(out=bg_sb[:], in_=bg3), 16)
            for i in range(NT_OWN):
                b.sync_all(S)
                b.cS.inc(S.dma_start(out=lnT_sb[:],
                                     in_=lnT3_dram[:, :, i * P:(i + 1) * P].rearrange("f p q -> p f q")), 16)
                b.cS.inc(S.dma_start(out=xb_sb[:], in_=x2in[i * P:(i + 1) * P, :]), 16)
                b.cS.inc(S.dma_start(out=lnx_sb[:], in_=lnf_dram[i * P:(i + 1) * P, :]), 16)
                b.cS.inc(S.dma_start(out=mht_sb[:], in_=mh_dram[i * P:(i + 1) * P, :]), 16)
                for cc in range(4):
                    b.sync_all(T)
                    b.sync_all(G)
                    b.cG.inc(G.dma_start(out=wg_sb[:],
                                         in_=wg3T[:, cc * 512:(cc + 1) * 512]
                                         .rearrange("(f p) c -> p f c", p=P)), 16)
                    b.wait(T, b.cG)
                    b.wait(T, b.cS)
                    for f in range(16):
                        T.matmul(ps_big[:, 0:512], lnT_sb[:, f, :], wg_sb[:, f, :],
                                 start=(f == 0), stop=False)
                    b.cP.inc(T.matmul(ps_big[:, 0:512], onesrow_bf[:],
                                      bg_sb[0:1, cc * 512:(cc + 1) * 512],
                                      start=False, stop=True))
                    b.wait(A, b.cP)
                    b.wait(A, b.cV)
                    A.activation(out=gate_sb[:, cc * 512:(cc + 1) * 512],
                                 in_=ps_big[:, 0:512], func=AF.Sigmoid)
                if DEBUG:
                    b.sync_all(S)
                    b.cS.inc(S.dma_start(out=dbg_gate[i * P:(i + 1) * P, :], in_=gate_sb[:]), 16)
                b.wait(V, b.cA)
                b.wait(V, b.cS)
                V.tensor_add(sq_sb[:], xb_sb[:], lnx_sb[:])
                V.tensor_mul(lnx_sb[:], gate_sb[:], mht_sb[:])
                V.tensor_add(sq_sb[:], sq_sb[:], lnx_sb[:])
                b.sync_all(S)
                b.cS.inc(S.dma_start(out=x3_dram[i * P:(i + 1) * P, :], in_=sq_sb[:]), 16)

        with ExitStack() as fctx:
            fe = fctx.enter_context
            wqf_sb = fe(nc.sbuf_tensor("wqf_sb", [P, 16, 512], BF16))
            keys_sb = fe(nc.sbuf_tensor("keys_sb", [P, 8, P], BF16))
            iota64_sb = fe(nc.sbuf_tensor("iota64_sb", [P, 64], F32))
            lnf_sb = fe(nc.sbuf_tensor("lnf_sb", [P, D], F32))
            qf_sb = fe(nc.sbuf_tensor("qf_sb", [P, 4, P], BF16))
            sims_sb = fe(nc.sbuf_tensor("sims_sb", [P, 8, P], F32))
            t8_sb = fe(nc.sbuf_tensor("t8_sb", [P, 8, 8], F32))
            i8u_sb = fe(nc.sbuf_tensor("i8u_sb", [P, 8, 8], U32))
            i8f_sb = fe(nc.sbuf_tensor("i8f_sb", [P, 8, 8], F32))
            cand_sb = fe(nc.sbuf_tensor("cand_sb", [P, 64], F32))
            emat_sb = fe(nc.sbuf_tensor("emat_sb", [P, 64], F32))
            f8_sb = fe(nc.sbuf_tensor("f8_sb", [P, 4, 8], F32))
            c8u_sb = fe(nc.sbuf_tensor("c8u_sb", [P, 8], U32))
            c8f_sb = fe(nc.sbuf_tensor("c8f_sb", [P, 8], F32))
            oh_sb = fe(nc.sbuf_tensor("oh_sb", [P, 64], F32))
            esel_sb = fe(nc.sbuf_tensor("esel_sb", [P, 32], F32))
            eidx_sb = fe(nc.sbuf_tensor("eidx_sb", [P, 32], U32))
            fsr_sb = fe(nc.sbuf_tensor("fsr_sb", [P, 4, 8], F32))
            hid_sb = fe(nc.sbuf_tensor("hid_sb", [P, 32], F32))
            coeff_sb = fe(nc.sbuf_tensor("coeff_sb", [P, 32], F32))
            gdn_sb = fe(nc.sbuf_tensor("gdn_sb", [P, D], F32))
            gup_sb = fe(nc.sbuf_tensor("gup_sb", [P, D], F32))
            acc_sb = fe(nc.sbuf_tensor("acc_sb", [P, D], F32))
            junk_sb = fe(nc.sbuf_tensor("junk_sb", [P, 72], F32))
            junk2_sb = fe(nc.sbuf_tensor("junk2_sb", [P, D], F32))
            b.sync_all(G)
            b.cG.inc(G.dma_start(out=wqf_sb[:], in_=wqf.rearrange("(f p) c -> p f c", p=P)), 16)
            b.cG.inc(G.dma_start(out=keys_sb[0:64, :, :], in_=keysT), 16)
            b.cG.inc(G.dma_start(out=keys_sb[64:128, :, :], in_=keysT), 16)
            b.cS.inc(S.dma_start(out=iota64_sb[:], in_=iota64), 16)
            b.wait(T, b.cG)
            b.wait(V, b.cS)
            env = dict(dbg=None,
                       ps_sims=[ps_scf[0][:, 0:P], ps_scf[1][:, 0:P]],
                       T=T, V=V, A=A, S=S, G=G, ident_f32=ident_f32, lnT_sb=lnT_sb,
                       ps_proj=ps_proj, ps_tr2=ps_tr2, e_down=e_down, e_up=e_up,
                       junk2=junk2_sb,
                       ffn_bufs=(wqf_sb, keys_sb, iota64_sb, lnf_sb, None, stat_sb,
                                 xb_sb, qf_sb, sims_sb, t8_sb, i8u_sb, i8f_sb,
                                 cand_sb, emat_sb, f8_sb, c8u_sb, c8f_sb, oh_sb,
                                 esel_sb, eidx_sb, fsr_sb, hid_sb, coeff_sb,
                                 gdn_sb, gup_sb, acc_sb, junk_sb))
            emit_ffn(b, env, x3_dram, y_out, 0)
        b.sync_all(S)
        b.sync_all(G)
    return nc


def _consts(core):
    half = core % 2
    off = 2048 * half
    ident = np.eye(P).astype(ml_dtypes.bfloat16)
    ident32 = np.eye(P).astype(np.float32)
    pp = np.arange(P)[:, None]
    ff = np.arange(P)[None, :]
    tri0 = (pp > ff).astype(ml_dtypes.bfloat16)
    tri8 = (pp <= ff).astype(ml_dtypes.bfloat16)
    slot = (np.arange(NT_CTX)[None, :] * P + pp).astype(np.float32)
    j_abs = off - HALO + slot
    cmask = (j_abs >= 0).astype(np.float32) if half == 0 else np.ones((P, NT_CTX), np.float32)
    iotaJ = j_abs.astype(np.float32)
    iotaI = (off + np.arange(NT_OWN)[None, :] * P + pp).astype(np.float32)
    onescol = np.ones((P, 1), np.float32)
    onesrow = np.ones((1, P), np.float32)
    return dict(identity=ident, identity32=ident32, tri0=tri0, tri8=tri8, cmask_in=cmask,
                iotaJ=iotaJ, iotaI=iotaI, onescol=onescol, onesrow=onesrow)


def kernel(**inputs):
    x = np.asarray(inputs["L0_attn_a_hidden_states"])
    cu = np.asarray(inputs["L0_attn_a_cu_seqlens_q"]).astype(np.int32)
    Wq = np.asarray(inputs["L0_attn_a_Wq"])
    Wk = np.asarray(inputs["L0_attn_a_Wk"])
    Wv = np.asarray(inputs["L0_attn_a_Wv"])
    Wo = np.asarray(inputs["L0_attn_a_Wo"])

    wq_perm = Wq.reshape(32, 64, D)[HEAD_ORDER].reshape(D, D)
    wqT = np.ascontiguousarray(wq_perm.T)
    wkT = np.ascontiguousarray(Wk.T)
    wvT = np.ascontiguousarray(Wv.T)
    woT = np.ascontiguousarray(Wo.T.reshape(32, 64, D)[HEAD_ORDER].reshape(D, D))
    cuq = np.zeros((1, 4), np.int32)
    cuq[0, :3] = cu
    Wqf = np.ascontiguousarray(np.asarray(inputs["L0_ffn_Wq"]))
    keysT = np.ascontiguousarray(np.asarray(inputs["L0_ffn_keys"]).transpose(3, 0, 2, 1)
                                 .reshape(64, 8, P))
    e_down_np = np.ascontiguousarray(np.asarray(inputs["L0_ffn_e_down"]))
    e_up_np = np.ascontiguousarray(np.asarray(inputs["L0_ffn_e_up"]))
    iota64_np = np.broadcast_to(np.arange(64, dtype=np.float32), (P, 64)).copy()
    wk3T = np.ascontiguousarray(np.asarray(inputs["L0_attn_b_Wk"]).T)
    bk3 = np.asarray(inputs["L0_attn_b_bk"]).reshape(1, 512)
    wv3T = np.ascontiguousarray(np.asarray(inputs["L0_attn_b_Wv"]).T)
    bv3 = np.asarray(inputs["L0_attn_b_bv"]).reshape(1, 512)
    wmb3T = np.ascontiguousarray(np.asarray(inputs["L0_attn_b_Wmb"]).T)
    bmb3 = np.asarray(inputs["L0_attn_b_bmb"]).reshape(1, D)
    mem_np = np.asarray(inputs["L0_attn_b_memory"]).reshape(16, 6, P, P)
    norm_np = np.asarray(inputs["L0_attn_b_norm"]).reshape(16, 6, P, 1)
    meml_np = np.ascontiguousarray(np.concatenate([mem_np, norm_np], axis=3))

    in_maps = []
    for c in range(8):
        bb, half = c // 2, c % 2
        own = x[bb, half * 2048:(half + 1) * 2048]
        if half == 0:
            halo = np.zeros((HALO, D), np.float32)
        else:
            halo = x[bb, 2048 - HALO:2048]
        x0h = np.ascontiguousarray(np.concatenate([halo, own], 0))
        m = dict(x0h=x0h, wqT=wqT, wkT=wkT, wvT=wvT, woT=woT, cuq=cuq,
                 wqf=Wqf, keysT=keysT, e_down=e_down_np, e_up=e_up_np,
                 iota64=iota64_np, wk3T=wk3T, bk3=bk3, wv3T=wv3T, bv3=bv3,
                 wmb3T=wmb3T, bmb3=bmb3, meml=meml_np)
        m.update(_consts(c))
        in_maps.append(m)

    nc = build_l1()
    res = run_bass_kernel_spmd(nc, in_maps, core_ids=list(range(8)))
    if OUT_KEY != "y_out":
        out = np.zeros((4, 4096, D), np.float32)
        for c in range(8):
            bb, half = c // 2, c % 2
            out[bb, half * 2048:(half + 1) * 2048] = res.results[c][OUT_KEY]
        return out

    delta_all = np.stack([res.results[c]["delta_out"] for c in range(8)], axis=3)
    dnorm_all = np.stack([res.results[c]["dnorm_out"] for c in range(8)], axis=3)
    wq3T = np.ascontiguousarray(np.asarray(inputs["L0_attn_b_Wq"]).T)
    bq3 = np.asarray(inputs["L0_attn_b_bq"]).reshape(1, D)
    wg3T = np.ascontiguousarray(np.asarray(inputs["L0_attn_b_Wg"]).T)
    bg3 = np.asarray(inputs["L0_attn_b_bg"]).reshape(1, D)
    ident32 = np.eye(P).astype(np.float32)
    onescol = np.ones((P, 1), np.float32)
    onesrow = np.ones((1, P), np.float32)
    in_maps2 = []
    for c in range(8):
        m2 = dict(x2in=np.ascontiguousarray(res.results[c]["x2_out"]),
                  meml=meml_np, delta_all=delta_all, dnorm_all=dnorm_all,
                  wq3T=wq3T, bq3=bq3, wg3T=wg3T, bg3=bg3,
                  wqf=Wqf, keysT=keysT, e_down=e_down_np, e_up=e_up_np,
                  iota64=iota64_np, identity32=ident32,
                  onescol=onescol, onesrow=onesrow)
        in_maps2.append(m2)
    nc2 = build_l2()
    res2 = run_bass_kernel_spmd(nc2, in_maps2, core_ids=list(range(8)))
    out = np.zeros((4, 4096, D), np.float32)
    for c in range(8):
        bb, half = c // 2, c % 2
        out[bb, half * 2048:(half + 1) * 2048] = res2.results[c]["y_out"]
    return out


# version 02 (final)
# speedup vs baseline: 1.2359x; 1.2359x over previous
DEBUG = False
OUT_KEY = "y_out"
FFN_STAGE = 9
import numpy as np
import ml_dtypes
from contextlib import ExitStack

import concourse.bass as bass
import concourse.mybir as mybir
from concourse.bass_utils import run_bass_kernel_spmd
from concourse.bass_core_rust import IndirectOffsetOnAxis

F32 = mybir.dt.float32
BF16 = mybir.dt.bfloat16
I32 = mybir.dt.int32
U32 = mybir.dt.uint32
AF = mybir.ActivationFunctionType
ALU = mybir.AluOpType
AX = mybir.AxisListType

P = 128
D = 2048
NT_OWN = 16
NT_CTX = 24
HALO = 1024
HD = 64
LN_EPS = 1e-5

HEAD_ORDER = []
for _j in range(4):
    for _i in range(4):
        HEAD_ORDER += [8 * _j + _i, 8 * _j + 4 + _i]
PAIR_HEADS = [(HEAD_ORDER[2 * m], HEAD_ORDER[2 * m + 1]) for m in range(16)]


class Ctr:
    def __init__(self, sem):
        self.sem = sem
        self.v = 0

    def inc(self, inst, n=1):
        inst.then_inc(self.sem, n)
        self.v += n


class B:
    def __init__(self, nc, ctx):
        self.nc = nc
        self.ctx = ctx
        e = ctx.enter_context
        self.cS = Ctr(e(nc.semaphore("semS")))
        self.cG = Ctr(e(nc.semaphore("semG")))
        self.cP = Ctr(e(nc.semaphore("semP")))
        self.cV = Ctr(e(nc.semaphore("semV")))
        self.cA = Ctr(e(nc.semaphore("semA")))
        self.all = [self.cS, self.cG, self.cP, self.cV, self.cA]
        self.V = StrictEng(nc.vector, self.cV)
        self.A = StrictEng(nc.scalar, self.cA)

    def sb(self, name, shape, dtype):
        return self.ctx.enter_context(self.nc.sbuf_tensor(name, shape, dtype))

    def ps(self, name, shape, dtype=F32):
        return self.ctx.enter_context(self.nc.psum_tensor(name, shape, dtype))

    def wait(self, eng, *ctrs):
        for c in ctrs:
            eng.wait_ge(c.sem, c.v)

    def sync_all(self, eng):
        for c in self.all:
            eng.wait_ge(c.sem, c.v)


_WRAP_OPS = {
    "tensor_copy", "tensor_tensor", "tensor_add", "tensor_sub", "tensor_mul",
    "tensor_max", "tensor_scalar", "tensor_single_scalar", "tensor_scalar_add",
    "tensor_scalar_sub", "tensor_scalar_mul", "tensor_scalar_max",
    "tensor_scalar_min", "tensor_relu", "tensor_reduce", "reduce_max",
    "reduce_sum", "scalar_tensor_tensor", "tensor_tensor_scan", "memset",
    "memzero", "select", "copy_predicated", "max", "max_index", "match_replace",
    "reciprocal", "transpose", "bn_stats", "bn_aggr", "stream_shuffle",
    "activation", "sign", "square", "sqrt", "mul", "copy", "add", "iota",
    "engine_nop",
}


class StrictEng:

    def __init__(self, eng, ctr):
        self._eng = eng
        self._ctr = ctr

    def __getattr__(self, name):
        f = getattr(self._eng, name)
        if name not in _WRAP_OPS:
            return f

        def wrapped(*a, **k):
            inst = f(*a, **k)
            self._ctr.inc(inst)
            self._eng.wait_ge(self._ctr.sem, self._ctr.v)
            return inst
        return wrapped


def emit_ln(b, x_sb, sq_scr, stat, ln_out):
    V, A = b.V, b.A
    sx, sxx, mean, varb, rstd, bias, tmp = (stat[:, i:i + 1] for i in range(7))
    V.memset(stat[:, 0:2], 0.0)
    b.sync_all(b.nc.scalar)
    A.activation(out=sq_scr[:], in_=x_sb[:], func=AF.Square, accum_out=sxx)
    A.activation(out=sq_scr[:], in_=x_sb[:], func=AF.Copy, accum_out=sx)
    b.wait(b.nc.vector, b.cA)
    V.tensor_scalar_mul(mean, sx, 1.0 / D)
    V.tensor_scalar_mul(tmp, sxx, 1.0 / D)
    V.scalar_tensor_tensor(out=varb, in0=mean, scalar=-1.0, in1=mean,
                           op0=ALU.mult, op1=ALU.mult)
    V.tensor_add(varb, varb, tmp)
    V.tensor_scalar_add(varb, varb, LN_EPS)
    b.wait(b.nc.scalar, b.cV)
    A.activation(out=rstd, in_=varb, func=AF.Ln)
    A.activation(out=rstd, in_=rstd, func=AF.Exp, scale=-0.5)
    b.wait(b.nc.vector, b.cA)
    V.scalar_tensor_tensor(out=bias, in0=mean, scalar=-1.0, in1=rstd,
                           op0=ALU.mult, op1=ALU.mult)
    b.wait(b.nc.scalar, b.cV)
    A.activation(out=ln_out[:], in_=x_sb[:], func=AF.Identity,
                 bias=bias, scale=rstd)


def emit_ffn(b, env, x_in, x_out, x_in_off):
    nc = b.nc
    T, V, A, S, G = env["T"], env["V"], env["A"], env["S"], env["G"]
    (wqf_sb, keys_sb, iota64_sb, lnf_sb, _unused, stat_sb, xb_sb, qf_sb,
     sims_sb, t8_sb, i8u_sb, i8f_sb, cand_sb, emat_sb, f8_sb, c8u_sb, c8f_sb,
     oh_sb, esel_sb, eidx_sb, fsr_sb, hid_sb, coeff_sb, gdn_sb, gup_sb,
     acc_sb, junk_sb) = env["ffn_bufs"]
    ps_proj = env["ps_proj"]
    ps_tr2 = env["ps_tr2"]
    e_down, e_up = env["e_down"], env["e_up"]

    for i in range(NT_OWN):
        rows = slice(x_in_off + i * P, x_in_off + (i + 1) * P)
        orows = slice(i * P, (i + 1) * P)
        b.sync_all(S)
        b.cS.inc(S.dma_start(out=xb_sb[:], in_=x_in[rows, :]), 16)
        b.wait(A, b.cS)
        b.wait(V, b.cS)
        if FFN_STAGE < 1:
            V.tensor_copy(acc_sb[:], xb_sb[:])
            b.sync_all(S)
            b.cS.inc(S.dma_start(out=x_out[orows, :], in_=acc_sb[:]), 16)
            continue
        emit_ln(b, xb_sb, lnf_sb, stat_sb, lnf_sb)
        b.wait(V, b.cA)
        if FFN_STAGE < 1.5:
            V.tensor_copy(acc_sb[:], xb_sb[:])
            b.sync_all(S)
            b.cS.inc(S.dma_start(out=x_out[orows, :], in_=acc_sb[:]), 16)
            continue
        for cb in range(4):
            pp = ps_proj[cb % 2]
            b.sync_all(T)
            for f in range(16):
                if cb == 0:
                    b.cP.inc(T.transpose(ps_tr2[f % 2], lnf_sb[:, f * P:(f + 1) * P],
                                         env["ident_f32"][:]))
                    b.wait(V, b.cP)
                    V.tensor_copy(env["lnT_sb"][:, f, :], ps_tr2[f % 2])
                    b.wait(T, b.cV)
            for f in range(16):
                b.cP.inc(T.matmul(pp, wqf_sb[:, f, cb * P:(cb + 1) * P],
                                  env["lnT_sb"][:, f, :], start=(f == 0), stop=(f == 15)))
            b.wait(V, b.cP)
            V.tensor_copy(qf_sb[:, cb, :], pp)
            b.wait(T, b.cV)
        ps_sims = env["ps_sims"]
        for h in range(4):
            b.sync_all(T)
            T.matmul(ps_sims[0], qf_sb[0:64, h, :], keys_sb[0:64, 2 * h, :],
                     start=True, stop=True, tile_position=(0, 0))
            b.cP.inc(T.matmul(ps_sims[1], qf_sb[64:128, h, :], keys_sb[64:128, 2 * h + 1, :],
                              start=True, stop=True, tile_position=(64, 0)))
            b.wait(V, b.cP)
            V.tensor_copy(sims_sb[:, 2 * h, :], ps_sims[0])
            V.tensor_copy(sims_sb[:, 2 * h + 1, :], ps_sims[1])
            b.wait(T, b.cV)
        if FFN_STAGE < 2:
            V.tensor_copy(acc_sb[:], xb_sb[:])
            b.sync_all(S)
            b.cS.inc(S.dma_start(out=x_out[orows, :], in_=acc_sb[:]), 16)
            continue
        for hs in range(8):
            V.max(out=t8_sb[:, hs, :], in_=sims_sb[:, hs, :])
            V.max_index(out=i8u_sb[:, hs, :], in_max=t8_sb[:, hs, :],
                        in_values=sims_sb[:, hs, :])
        V.tensor_copy(i8f_sb[:], i8u_sb[:])
        for h in range(4):
            s1 = t8_sb[:, 2 * h, :]
            s2 = t8_sb[:, 2 * h + 1, :]
            i1f = i8f_sb[:, 2 * h, :]
            i2f = i8f_sb[:, 2 * h + 1, :]
            V.tensor_scalar_mul(junk_sb[:, 0:8], i1f, 128.0)
            for a in range(8):
                V.tensor_scalar(cand_sb[:, a * 8:(a + 1) * 8], s2, s1[:, a:a + 1],
                                None, op0=ALU.add)
                V.tensor_scalar(emat_sb[:, a * 8:(a + 1) * 8], i2f,
                                junk_sb[:, a:a + 1], None, op0=ALU.add)
            V.max(out=f8_sb[:, h, :], in_=cand_sb[:])
            V.max_index(out=c8u_sb[:], in_max=f8_sb[:, h, :], in_values=cand_sb[:])
            V.tensor_copy(c8f_sb[:], c8u_sb[:])
            for r in range(8):
                V.tensor_scalar(oh_sb[:], iota64_sb[:], c8f_sb[:, r:r + 1], None,
                                op0=ALU.is_equal)
                V.scalar_tensor_tensor(out=junk_sb[:, 8:72], in0=oh_sb[:], scalar=1.0,
                                       in1=emat_sb[:], op0=ALU.mult, op1=ALU.mult,
                                       accum_out=esel_sb[:, h * 8 + r:h * 8 + r + 1])
            V.tensor_scalar_max(fsr_sb[:, h, :], f8_sb[:, h, :], 0.0)
        V.tensor_scalar_max(esel_sb[:], esel_sb[:], 0.0)
        V.tensor_scalar_min(esel_sb[:], esel_sb[:], 16383.0)
        V.tensor_copy(eidx_sb[:], esel_sb[:])
        if env.get("dbg") is not None and i == 0:
            d = env["dbg"]
            b.sync_all(S)
            b.cS.inc(S.dma_start(out=d["sims"], in_=sims_sb[:]), 16)
            b.cS.inc(S.dma_start(out=d["t8"], in_=t8_sb[:]), 16)
            b.cS.inc(S.dma_start(out=d["i8u"], in_=i8u_sb[:]), 16)
            b.cS.inc(S.dma_start(out=d["eidx"], in_=eidx_sb[:]), 16)
            b.cS.inc(S.dma_start(out=d["f8"], in_=f8_sb[:]), 16)
        if FFN_STAGE < 3:
            V.tensor_copy(acc_sb[:], xb_sb[:])
            b.sync_all(S)
            b.cS.inc(S.dma_start(out=x_out[orows, :], in_=acc_sb[:]), 16)
            continue
        V.memset(hid_sb[:], 0.0)
        gbuf = [gdn_sb, gup_sb]
        gready = [0, 0]
        vdone = [b.cV.v, b.cV.v]
        b.sync_all(G)
        b.cG.inc(G.indirect_dma_start(
            out=gbuf[0][:], out_offset=None, in_=e_down[0:P, :],
            in_offset=IndirectOffsetOnAxis(ap=eidx_sb[:, 0:1], axis=0)), 16)
        gready[0] = b.cG.v
        for s in range(32):
            if s + 1 < 32:
                nb = (s + 1) % 2
                G.wait_ge(b.cV.sem, vdone[nb])
                b.cG.inc(G.indirect_dma_start(
                    out=gbuf[nb][:], out_offset=None, in_=e_down[0:P, :],
                    in_offset=IndirectOffsetOnAxis(ap=eidx_sb[:, s + 1:s + 2], axis=0)), 16)
                gready[nb] = b.cG.v
            V.wait_ge(b.cG.sem, gready[s % 2]) if False else nc_wait_v(b, gready[s % 2])
            V.scalar_tensor_tensor(out=junk2_dummy(env), in0=gbuf[s % 2][:], scalar=1.0,
                                   in1=lnf_sb[:], op0=ALU.mult, op1=ALU.mult,
                                   accum_out=hid_sb[:, s:s + 1])
            vdone[s % 2] = b.cV.v
        if FFN_STAGE < 4:
            V.tensor_copy(acc_sb[:], xb_sb[:])
            b.sync_all(S)
            b.cS.inc(S.dma_start(out=x_out[orows, :], in_=acc_sb[:]), 16)
            continue
        b.wait(A, b.cV)
        A.activation(out=hid_sb[:], in_=hid_sb[:], func=AF.Silu)
        b.wait(V, b.cA)
        V.tensor_mul(coeff_sb[:], hid_sb[:], fsr_sb[:].rearrange("p h r -> p (h r)"))
        V.memset(acc_sb[:], 0.0)
        b.sync_all(G)
        b.cG.inc(G.indirect_dma_start(
            out=gbuf[0][:], out_offset=None, in_=e_up[0:P, :],
            in_offset=IndirectOffsetOnAxis(ap=eidx_sb[:, 0:1], axis=0)), 16)
        gready[0] = b.cG.v
        for s in range(32):
            if s + 1 < 32:
                nb = (s + 1) % 2
                G.wait_ge(b.cV.sem, vdone[nb])
                b.cG.inc(G.indirect_dma_start(
                    out=gbuf[nb][:], out_offset=None, in_=e_up[0:P, :],
                    in_offset=IndirectOffsetOnAxis(ap=eidx_sb[:, s + 1:s + 2], axis=0)), 16)
                gready[nb] = b.cG.v
            nc_wait_v(b, gready[s % 2])
            V.scalar_tensor_tensor(out=acc_sb[:], in0=gbuf[s % 2][:], scalar=coeff_sb[:, s:s + 1],
                                   in1=acc_sb[:], op0=ALU.mult, op1=ALU.add)
            vdone[s % 2] = b.cV.v
        V.tensor_add(acc_sb[:], acc_sb[:], xb_sb[:])
        b.sync_all(S)
        b.cS.inc(S.dma_start(out=x_out[orows, :], in_=acc_sb[:]), 16)


def junk2_dummy(env):
    return env["junk2"][:]


def nc_wait_v(b, val):
    b.nc.vector.wait_ge(b.cG.sem, val)


def build_l1():
    nc = bass.Bass("TRN2", target_bir_lowering=False)
    dt = nc.dram_tensor
    x0h = dt("x0h", [NT_CTX * P, D], F32, kind="ExternalInput").ap()
    wqT = dt("wqT", [D, D], F32, kind="ExternalInput").ap()
    wkT = dt("wkT", [D, 512], F32, kind="ExternalInput").ap()
    wvT = dt("wvT", [D, 512], F32, kind="ExternalInput").ap()
    woT = dt("woT", [D, D], F32, kind="ExternalInput").ap()
    cuq = dt("cuq", [1, 4], I32, kind="ExternalInput").ap()
    identity = dt("identity", [P, P], BF16, kind="ExternalInput").ap()
    identity32 = dt("identity32", [P, P], F32, kind="ExternalInput").ap()
    tri0 = dt("tri0", [P, P], BF16, kind="ExternalInput").ap()
    tri8 = dt("tri8", [P, P], BF16, kind="ExternalInput").ap()
    cmask_in = dt("cmask_in", [P, NT_CTX], F32, kind="ExternalInput").ap()
    iotaJ = dt("iotaJ", [P, NT_CTX], F32, kind="ExternalInput").ap()
    iotaI = dt("iotaI", [P, NT_OWN], F32, kind="ExternalInput").ap()
    onescol = dt("onescol", [P, 1], F32, kind="ExternalInput").ap()
    onesrow = dt("onesrow", [1, P], F32, kind="ExternalInput").ap()
    x1_out = dt("x1_out", [NT_OWN * P, D], F32, kind="ExternalOutput").ap()
    x2_out = dt("x2_out", [NT_OWN * P, D], F32, kind="ExternalOutput").ap()
    wqf = dt("wqf", [D, 512], F32, kind="ExternalInput").ap()
    keysT = dt("keysT", [64, 8, P], F32, kind="ExternalInput").ap()
    e_down = dt("e_down", [16384, D], F32, kind="ExternalInput").ap()
    e_up = dt("e_up", [16384, D], F32, kind="ExternalInput").ap()
    iota64 = dt("iota64", [P, 64], F32, kind="ExternalInput").ap()
    wk3T = dt("wk3T", [D, 512], F32, kind="ExternalInput").ap()
    bk3 = dt("bk3", [1, 512], F32, kind="ExternalInput").ap()
    wv3T = dt("wv3T", [D, 512], F32, kind="ExternalInput").ap()
    bv3 = dt("bv3", [1, 512], F32, kind="ExternalInput").ap()
    wmb3T = dt("wmb3T", [D, D], F32, kind="ExternalInput").ap()
    bmb3 = dt("bmb3", [1, D], F32, kind="ExternalInput").ap()
    meml = dt("meml", [16, 6, P, 129], F32, kind="ExternalInput").ap()
    delta_out = dt("delta_out", [16, 6, P, P], F32, kind="ExternalOutput").ap()
    dnorm_out = dt("dnorm_out", [6, P, 16], F32, kind="ExternalOutput").ap()
    mb3_dram = dt("mb3_dram", [NT_OWN * P, D], BF16, kind="Internal").ap()
    dramkind = "ExternalOutput" if DEBUG else "Internal"
    lnT_dram = dt("lnT_dram", [16, P, NT_CTX * P], BF16, kind=dramkind).ap()
    q_dram = dt("q_dram", [16, P, NT_OWN * P], BF16, kind=dramkind).ap()
    aT_dram = dt("aT_dram", [16, P, NT_OWN * P], BF16, kind=dramkind).ap()
    if DEBUG:
        dbg_b = dict(
            k3=dt("dbgb_k3", [P, 4, D], BF16, kind="ExternalOutput").ap(),
            v3=dt("dbgb_v3", [P, 16, 512], BF16, kind="ExternalOutput").ap(),
            dptok=dt("dbgb_dptok", [P, 16, 768], BF16, kind="ExternalOutput").ap(),
            snorm=dt("dbgb_snorm", [P, 16], F32, kind="ExternalOutput").ap(),
            rnorm=dt("dbgb_rnorm", [P, 16], F32, kind="ExternalOutput").ap(),
            wmv=dt("dbgb_wmv", [P, 16, P], BF16, kind="ExternalOutput").ap(),
            pvall=dt("dbgb_pvall", [P, 16, P], BF16, kind="ExternalOutput").ap(),
            mbh=dt("dbgb_mbh", [P, 16, P], BF16, kind="ExternalOutput").ap(),
            numall=dt("dbgb_numall", [P, 16, P], BF16, kind="ExternalOutput").ap(),
            denall=dt("dbgb_denall", [P, 16, 2], F32, kind="ExternalOutput").ap(),
            coefs=dt("dbgb_coefs", [P, 16, 4], BF16, kind="ExternalOutput").ap(),
            dpfp=dt("dbgb_dpfp", [P, 6, D], BF16, kind="ExternalOutput").ap(),
        )
        dbg_f = dict(
            sims=dt("dbgf_sims", [P, 8, P], F32, kind="ExternalOutput").ap(),
            t8=dt("dbgf_t8", [P, 8, 8], F32, kind="ExternalOutput").ap(),
            i8u=dt("dbgf_i8u", [P, 8, 8], U32, kind="ExternalOutput").ap(),
            eidx=dt("dbgf_eidx", [P, 32], U32, kind="ExternalOutput").ap(),
            f8=dt("dbgf_f8", [P, 4, 8], F32, kind="ExternalOutput").ap(),
        )
        dbg_norm = dt("dbg_norm", [1, 64], F32, kind="ExternalOutput").ap()
        dbg_negc = dt("dbg_negc", [P, 32], F32, kind="ExternalOutput").ap()
        dbg_kpair = dt("dbg_kpair", [P, 4, NT_CTX * P], BF16, kind="ExternalOutput").ap()
        dbg_vaug = dt("dbg_vaug", [P, 8, NT_CTX, 130], BF16, kind="ExternalOutput").ap()
        dbg_probs = dt("dbg_probs", [2, P, 9 * P], BF16, kind="ExternalOutput").ap()

    with ExitStack() as ctx:
        b = B(nc, ctx)
        T, V, A, S, G = nc.tensor, b.V, b.A, nc.sync, nc.gpsimd

        ident_sb = b.sb("ident_sb", [P, P], BF16)
        ident_f32 = b.sb("ident_f32", [P, P], F32)
        tri0_sb = b.sb("tri0_sb", [P, P], BF16)
        tri8_sb = b.sb("tri8_sb", [P, P], BF16)
        cmask_sb = b.sb("cmask_sb", [P, NT_CTX], F32)
        bprime_sb = b.sb("bprime_sb", [P, NT_CTX], F32)
        iotaJ_sb = b.sb("iotaJ_sb", [P, NT_CTX], F32)
        iotaI_sb = b.sb("iotaI_sb", [P, NT_OWN], F32)
        aq_sb = b.sb("aq_sb", [P, NT_OWN], F32)
        aqU_sb = b.sb("aqU_sb", [P, NT_OWN], mybir.dt.uint8)
        ones_sb = b.sb("ones_sb", [P, 1], F32)
        onesrow_sb = b.sb("onesrow_sb", [1, P], F32)
        cu_sb = b.sb("cu_sb", [1, 4], I32)
        lval_sb = b.sb("lval_sb", [1, 4], F32)
        lb_sb = b.sb("lb_sb", [P, 1], F32)
        negc_sb = b.sb("negc_sb", [P, 32], F32)
        norm_sb = b.sb("norm_sb", [1, 64], F32)
        xb_sb = b.sb("xb", [P, D], F32)
        sq_sb = b.sb("sq_sb", [P, D], F32)
        stat_sb = b.sb("stat_sb", [P, 8], F32)
        lnT_sb = b.sb("lnT_sb", [P, 16, P], BF16)
        attn_stack = ExitStack()
        kpair_sb = attn_stack.enter_context(nc.sbuf_tensor("kpair_sb", [P, 4, NT_CTX * P], BF16))
        vaug_sb = attn_stack.enter_context(nc.sbuf_tensor("vaug_sb", [P, 8, NT_CTX, 130], BF16))
        wk_sb = attn_stack.enter_context(nc.sbuf_tensor("wk_sb", [P, 16, 512], BF16))
        wv_sb = attn_stack.enter_context(nc.sbuf_tensor("wv_sb", [P, 16, 512], BF16))

        ps_scf = [b.ps(f"ps_scf{i}", [P, 1536], F32) for i in range(2)]
        ps_sc = [ps_scf[i][:, 0:9 * P] for i in range(2)]
        ps_av = [ps_scf[i][:, 1152:1282] for i in range(2)]
        ps_midf = b.ps("ps_midf", [P, 512], F32)
        ps_tr2 = [ps_midf[:, 0:128], ps_midf[:, 128:256]]
        ps_proj = [ps_midf[:, 256:384], ps_midf[:, 384:512]]
        ps_big = b.ps("ps_big", [P, 512], F32)
        ps_nrm = ps_big[0:1, :]

        for dst, src in ((ident_sb, identity), (ident_f32, identity32),
                         (tri0_sb, tri0), (tri8_sb, tri8),
                         (cmask_sb, cmask_in), (iotaJ_sb, iotaJ), (iotaI_sb, iotaI),
                         (ones_sb, onescol), (onesrow_sb, onesrow), (cu_sb, cuq)):
            b.cS.inc(S.dma_start(out=dst[:], in_=src), 16)
        b.cG.inc(G.dma_start(out=wk_sb[:], in_=wkT.rearrange("(f p) c -> p f c", p=P)), 16)
        b.cG.inc(G.dma_start(out=wv_sb[:], in_=wvT.rearrange("(f p) c -> p f c", p=P)), 16)
        b.wait(V, b.cS)
        V.tensor_sub(cu_sb[0:1, 2:3], cu_sb[0:1, 1:2], cu_sb[0:1, 0:1])
        V.tensor_copy(lval_sb[0:1, 0:1], cu_sb[0:1, 2:3])
        b.wait(T, b.cV)
        b.cP.inc(T.matmul(ps_proj[0][:, 0:1], onesrow_sb[:], lval_sb[0:1, 0:1],
                          start=True, stop=True))
        b.wait(V, b.cP)
        V.tensor_copy(lb_sb[:], ps_proj[0][:, 0:1])
        V.tensor_tensor(aq_sb[:], iotaI_sb[:], lb_sb[:].to_broadcast([P, NT_OWN]), op=ALU.is_lt)
        V.tensor_tensor(bprime_sb[:], iotaJ_sb[:], lb_sb[:].to_broadcast([P, NT_CTX]), op=ALU.is_ge)
        V.tensor_mul(bprime_sb[:], bprime_sb[:], cmask_sb[:])
        V.tensor_copy(aqU_sb[:], aq_sb[:])
        V.memset(norm_sb[:], 0.0)

        for t in range(NT_CTX):
            b.sync_all(S)
            b.cS.inc(S.dma_start(out=xb_sb[:], in_=x0h[t * P:(t + 1) * P, :]), 16)
            b.wait(A, b.cS)
            emit_ln(b, xb_sb, sq_sb, stat_sb, sq_sb)
            b.wait(T, b.cA)
            for f in range(16):
                ptr = ps_tr2[f % 2]
                b.cP.inc(T.transpose(ptr, sq_sb[:, f * P:(f + 1) * P], ident_f32[:]))
                b.wait(V, b.cP)
                V.tensor_copy(lnT_sb[:, f, :], ptr)
                b.wait(T, b.cV)
            b.sync_all(S)
            b.cS.inc(S.dma_start(out=lnT_dram[:, :, t * P:(t + 1) * P].rearrange("f p q -> p f q"),
                                 in_=lnT_sb[:]), 16)
            b.wait(T, b.cG)
            for j in range(4):
                pp = ps_proj[j % 2]
                b.sync_all(T)
                for f in range(16):
                    b.cP.inc(T.matmul(pp[:], wk_sb[:, f, j * P:(j + 1) * P], lnT_sb[:, f, :],
                                      start=(f == 0), stop=(f == 15)))
                b.wait(V, b.cP)
                V.tensor_copy(kpair_sb[:, j, t * P:(t + 1) * P], pp[:])
            b.sync_all(T)
            for f in range(16):
                b.cP.inc(T.matmul(ps_big[:], lnT_sb[:, f, :], wv_sb[:, f, :],
                                  start=(f == 0), stop=(f == 15)))
            b.wait(V, b.cP)
            for h in range(8):
                vsl = ps_big[:, h * HD:(h + 1) * HD]
                V.tensor_scalar(vaug_sb[:, h, t, 0:64], vsl, cmask_sb[:, t:t + 1], None, op0=ALU.mult)
                V.tensor_scalar(vaug_sb[:, h, t, 65:129], vsl, bprime_sb[:, t:t + 1], None, op0=ALU.mult)
                V.tensor_copy(vaug_sb[:, h, t, 64:65], cmask_sb[:, t:t + 1])
                V.tensor_copy(vaug_sb[:, h, t, 129:130], bprime_sb[:, t:t + 1])
            V.engine_nop()
            b.wait(T, b.cV)

        for j in range(4):
            for blk in range(6):
                cols = slice(blk * 512, (blk + 1) * 512)
                b.sync_all(A)
                A.activation(out=sq_sb[:, 0:512], in_=kpair_sb[:, j, cols], func=AF.Square)
                b.wait(T, b.cA)
                b.cP.inc(T.matmul(ps_nrm[0:1, 0:512], ones_sb[0:64, :], sq_sb[0:64, 0:512],
                                  start=True, stop=True))
                b.wait(V, b.cP)
                V.reduce_max(stat_sb[0:1, 0:1], ps_nrm[0:1, 0:512], axis=AX.X)
                V.tensor_max(norm_sb[0:1, 32 + 2 * j:32 + 2 * j + 1],
                             norm_sb[0:1, 32 + 2 * j:32 + 2 * j + 1], stat_sb[0:1, 0:1])
                V.engine_nop()
                b.wait(T, b.cV)
                b.cP.inc(T.matmul(ps_nrm[0:1, 0:512], ones_sb[64:128, :], sq_sb[64:128, 0:512],
                                  start=True, stop=True))
                b.wait(V, b.cP)
                V.reduce_max(stat_sb[0:1, 0:1], ps_nrm[0:1, 0:512], axis=AX.X)
                V.tensor_max(norm_sb[0:1, 32 + 2 * j + 1:32 + 2 * j + 2],
                                      norm_sb[0:1, 32 + 2 * j + 1:32 + 2 * j + 2], stat_sb[0:1, 0:1])

        with ExitStack() as qctx:
            wq_sb = qctx.enter_context(nc.sbuf_tensor("wq_sb", [P, 16, 16, P], BF16))
            qall_sb = qctx.enter_context(nc.sbuf_tensor("qall_sb", [P, 16, P], BF16))
            qsq_sb = qctx.enter_context(nc.sbuf_tensor("qsq_sb", [P, P], F32))
            b.sync_all(G)
            b.cG.inc(G.dma_start(out=wq_sb[:],
                                 in_=wqT.rearrange("(f p) (m c) -> p f m c", p=P, c=P)), 16)
            for i in range(NT_OWN):
                t = 8 + i
                b.sync_all(S)
                b.cS.inc(S.dma_start(out=lnT_sb[:],
                                     in_=lnT_dram[:, :, t * P:(t + 1) * P].rearrange("f p q -> p f q")), 16)
                b.wait(T, b.cS, b.cG)
                for m in range(16):
                    pp = ps_proj[m % 2]
                    b.sync_all(T)
                    for f in range(16):
                        b.cP.inc(T.matmul(pp[:], wq_sb[:, f, m, :], lnT_sb[:, f, :],
                                          start=(f == 0), stop=(f == 15)))
                    b.wait(V, b.cP)
                    V.tensor_scalar_mul(qall_sb[:, m, :], pp[:], 0.125)
                    b.wait(A, b.cV)
                    A.activation(out=qsq_sb[:], in_=qall_sb[:, m, :], func=AF.Square)
                    b.wait(T, b.cA)
                    hl, hh = PAIR_HEADS[m]
                    b.cP.inc(T.matmul(ps_nrm[0:1, 0:P], ones_sb[0:64, :], qsq_sb[0:64, :],
                                      start=True, stop=True))
                    b.wait(V, b.cP)
                    V.reduce_max(stat_sb[0:1, 1:2], ps_nrm[0:1, 0:P], axis=AX.X)
                    V.tensor_max(norm_sb[0:1, hl:hl + 1], norm_sb[0:1, hl:hl + 1],
                                          stat_sb[0:1, 1:2])
                    b.wait(T, b.cV)
                    b.cP.inc(T.matmul(ps_nrm[0:1, 0:P], ones_sb[64:128, :], qsq_sb[64:128, :],
                                      start=True, stop=True))
                    b.wait(V, b.cP)
                    V.reduce_max(stat_sb[0:1, 1:2], ps_nrm[0:1, 0:P], axis=AX.X)
                    V.tensor_max(norm_sb[0:1, hh:hh + 1], norm_sb[0:1, hh:hh + 1],
                                          stat_sb[0:1, 1:2])
                b.sync_all(S)
                b.cS.inc(S.dma_start(out=q_dram[:, :, i * P:(i + 1) * P].rearrange("m p q -> p m q"),
                                     in_=qall_sb[:]), 16)

        b.sync_all(V)
        for h in range(32):
            V.tensor_mul(sq_sb[0:1, h:h + 1], norm_sb[0:1, h:h + 1],
                                  norm_sb[0:1, 32 + h // 4:33 + h // 4])
        b.wait(A, b.cV)
        A.activation(out=sq_sb[0:1, 32:64], in_=sq_sb[0:1, 0:32], func=AF.Ln)
        A.activation(out=sq_sb[0:1, 32:64], in_=sq_sb[0:1, 32:64], func=AF.Exp, scale=0.5)
        b.wait(V, b.cA)
        V.tensor_scalar_mul(sq_sb[0:1, 32:64], sq_sb[0:1, 32:64], -1.0)
        b.wait(T, b.cV)
        b.cP.inc(T.matmul(ps_big[:, 0:32], onesrow_sb[:], sq_sb[0:1, 32:64],
                          start=True, stop=True))
        b.wait(V, b.cP)
        V.tensor_copy(negc_sb[:], ps_big[:, 0:32])
        if DEBUG:
            b.sync_all(S)
            b.cS.inc(S.dma_start(out=dbg_norm, in_=norm_sb[:]), 16)
            b.cS.inc(S.dma_start(out=dbg_negc, in_=negc_sb[:]), 16)
            b.cS.inc(S.dma_start(out=dbg_kpair, in_=kpair_sb[:]), 16)
            b.cS.inc(S.dma_start(out=dbg_vaug, in_=vaug_sb[:]), 16)

        with ExitStack() as actx:
            qtile_sb = actx.enter_context(nc.sbuf_tensor("qtile_sb", [P, 16, P], BF16))
            probs_sb = [actx.enter_context(nc.sbuf_tensor(f"probs{i}", [P, 9 * P], BF16))
                        for i in range(2)]
            avf_sb = actx.enter_context(nc.sbuf_tensor("avf_sb", [P, 130], F32))
            avsel_sb = actx.enter_context(nc.sbuf_tensor("avsel_sb", [P, HD], F32))
            den_sb = actx.enter_context(nc.sbuf_tensor("den_sb", [P, 1], F32))
            atok_sb = actx.enter_context(nc.sbuf_tensor("atok_sb", [P, HD], F32))
            aT_sb = actx.enter_context(nc.sbuf_tensor("aT_sb", [P, 16, P], BF16))
            for qt in range(NT_OWN):
                b.sync_all(S)
                b.cS.inc(S.dma_start(out=qtile_sb[:],
                                     in_=q_dram[:, :, qt * P:(qt + 1) * P].rearrange("m p q -> p m q")), 16)
                for m in range(16):
                    j = m // 4
                    hl, hh = PAIR_HEADS[m]
                    kcol0 = qt * P
                    b.sync_all(T)
                    for kk in range(9):
                        ksl = slice(kcol0 + kk * P, kcol0 + (kk + 1) * P)
                        psl = slice(kk * P, (kk + 1) * P)
                        T.matmul(ps_sc[0][:, psl], kpair_sb[0:64, j, ksl], qtile_sb[0:64, m, :],
                                 start=True, stop=True, tile_position=(0, 0))
                        b.cP.inc(T.matmul(ps_sc[1][:, psl], kpair_sb[64:128, j, ksl],
                                          qtile_sb[64:128, m, :], start=True, stop=True,
                                          tile_position=(64, 0)))
                    b.wait(A, b.cP)
                    for i, h in enumerate((hl, hh)):
                        A.activation(out=probs_sb[i][:], in_=ps_sc[i][:],
                                              func=AF.Exp, bias=negc_sb[:, h:h + 1], scale=1.0)
                    b.wait(V, b.cA)
                    for i in range(2):
                        V.tensor_mul(probs_sb[i][:, 0:P], probs_sb[i][:, 0:P], tri0_sb[:])
                        V.tensor_mul(probs_sb[i][:, 8 * P:9 * P], probs_sb[i][:, 8 * P:9 * P], tri8_sb[:])
                    V.engine_nop()
                    if DEBUG and qt == 4 and m == 0:
                        b.sync_all(S)
                        for i in range(2):
                            b.cS.inc(S.dma_start(out=dbg_probs[i], in_=probs_sb[i][:]), 16)
                    b.wait(T, b.cV)
                    for i, h in enumerate((hl, hh)):
                        kvh = h // 4
                        for kk in range(9):
                            kt_g = qt + kk
                            b.cP.inc(T.matmul(ps_av[i][:], probs_sb[i][:, kk * P:(kk + 1) * P],
                                              vaug_sb[:, kvh, kt_g, :],
                                              start=(kk == 0), stop=(kk == 8)))
                    b.wait(V, b.cP)
                    for i, h in enumerate((hl, hh)):
                        V.tensor_copy(avf_sb[:], ps_av[i][:])
                        V.engine_nop()
                        V.select(den_sb[:], aqU_sb[:, qt:qt + 1], avf_sb[:, 64:65], avf_sb[:, 129:130])
                        V.select(avsel_sb[:], aqU_sb[:, qt:qt + 1].to_broadcast([P, HD]),
                                 avf_sb[:, 0:64], avf_sb[:, 65:129])
                        V.reciprocal(den_sb[:], den_sb[:])
                        b.wait(V, b.cV)
                        V.tensor_mul(atok_sb[:], avsel_sb[:],
                                              den_sb[:].to_broadcast([P, HD]))
                        b.wait(T, b.cV)
                        b.cP.inc(T.transpose(ps_tr2[i][0:64, :], atok_sb[:], ident_f32[:]))
                        b.wait(V, b.cP)
                        base = 0 if i == 0 else 64
                        V.tensor_copy(aT_sb[base:base + 64, m, :], ps_tr2[i][0:64, :])
                        b.wait(T, b.cV)
                b.sync_all(S)
                b.cS.inc(S.dma_start(out=aT_dram[:, :, qt * P:(qt + 1) * P].rearrange("m p q -> p m q"),
                                     in_=aT_sb[:]), 16)

        attn_stack.close()
        with ExitStack() as octx:
            wo_sb = octx.enter_context(nc.sbuf_tensor("wo_sb", [P, 16, 512], BF16))
            aTt_sb = octx.enter_context(nc.sbuf_tensor("aTt_sb", [P, 16, P], BF16))
            xo_sb = octx.enter_context(nc.sbuf_tensor("xo_sb", [P, 512], F32))
            x1_sb = octx.enter_context(nc.sbuf_tensor("x1_sb", [P, 512], F32))
            for cch in range(4):
                ccols = slice(cch * 512, (cch + 1) * 512)
                b.sync_all(G)
                b.cG.inc(G.dma_start(out=wo_sb[:],
                                     in_=woT[:, ccols].rearrange("(f p) c -> p f c", p=P)), 16)
                for qt in range(NT_OWN):
                    b.sync_all(S)
                    b.cS.inc(S.dma_start(out=aTt_sb[:],
                                         in_=aT_dram[:, :, qt * P:(qt + 1) * P].rearrange("m p q -> p m q")), 16)
                    b.cS.inc(S.dma_start(out=xo_sb[:],
                                         in_=x0h[(8 + qt) * P:(9 + qt) * P, ccols]), 16)
                    b.wait(T, b.cS, b.cG)
                    for m in range(16):
                        b.cP.inc(T.matmul(ps_big[:], aTt_sb[:, m, :], wo_sb[:, m, :],
                                          start=(m == 0), stop=(m == 15)))
                    b.wait(V, b.cP)
                    V.tensor_add(x1_sb[:], xo_sb[:], ps_big[:])
                    b.sync_all(S)
                    b.cS.inc(S.dma_start(out=x1_out[qt * P:(qt + 1) * P, ccols], in_=x1_sb[:]), 16)
        with ExitStack() as fctx:
            fe = fctx.enter_context
            wqf_sb = fe(nc.sbuf_tensor("wqf_sb", [P, 16, 512], BF16))
            keys_sb = fe(nc.sbuf_tensor("keys_sb", [P, 8, P], BF16))
            iota64_sb = fe(nc.sbuf_tensor("iota64_sb", [P, 64], F32))
            lnf_sb = fe(nc.sbuf_tensor("lnf_sb", [P, D], F32))
            qf_sb = fe(nc.sbuf_tensor("qf_sb", [P, 4, P], BF16))
            sims_sb = fe(nc.sbuf_tensor("sims_sb", [P, 8, P], F32))
            t8_sb = fe(nc.sbuf_tensor("t8_sb", [P, 8, 8], F32))
            i8u_sb = fe(nc.sbuf_tensor("i8u_sb", [P, 8, 8], U32))
            i8f_sb = fe(nc.sbuf_tensor("i8f_sb", [P, 8, 8], F32))
            cand_sb = fe(nc.sbuf_tensor("cand_sb", [P, 64], F32))
            emat_sb = fe(nc.sbuf_tensor("emat_sb", [P, 64], F32))
            f8_sb = fe(nc.sbuf_tensor("f8_sb", [P, 4, 8], F32))
            c8u_sb = fe(nc.sbuf_tensor("c8u_sb", [P, 8], U32))
            c8f_sb = fe(nc.sbuf_tensor("c8f_sb", [P, 8], F32))
            oh_sb = fe(nc.sbuf_tensor("oh_sb", [P, 64], F32))
            esel_sb = fe(nc.sbuf_tensor("esel_sb", [P, 32], F32))
            eidx_sb = fe(nc.sbuf_tensor("eidx_sb", [P, 32], U32))
            fsr_sb = fe(nc.sbuf_tensor("fsr_sb", [P, 4, 8], F32))
            hid_sb = fe(nc.sbuf_tensor("hid_sb", [P, 32], F32))
            coeff_sb = fe(nc.sbuf_tensor("coeff_sb", [P, 32], F32))
            gdn_sb = fe(nc.sbuf_tensor("gdn_sb", [P, D], F32))
            gup_sb = fe(nc.sbuf_tensor("gup_sb", [P, D], F32))
            acc_sb = fe(nc.sbuf_tensor("acc_sb", [P, D], F32))
            junk_sb = fe(nc.sbuf_tensor("junk_sb", [P, 72], F32))
            junk2_sb = fe(nc.sbuf_tensor("junk2_sb", [P, D], F32))
            b.sync_all(G)
            b.cG.inc(G.dma_start(out=wqf_sb[:], in_=wqf.rearrange("(f p) c -> p f c", p=P)), 16)
            b.cG.inc(G.dma_start(out=keys_sb[0:64, :, :], in_=keysT), 16)
            b.cG.inc(G.dma_start(out=keys_sb[64:128, :, :], in_=keysT), 16)
            b.cS.inc(S.dma_start(out=iota64_sb[:], in_=iota64), 16)
            b.wait(T, b.cG)
            b.wait(V, b.cS)
            env = dict(dbg=dbg_f if DEBUG else None,
                       ps_sims=[ps_scf[0][:, 0:P], ps_scf[1][:, 0:P]],
                       T=T, V=V, A=A, S=S, G=G, ident_f32=ident_f32, lnT_sb=lnT_sb,
                       ps_proj=ps_proj, ps_tr2=ps_tr2, e_down=e_down, e_up=e_up,
                       junk2=junk2_sb,
                       ffn_bufs=(wqf_sb, keys_sb, iota64_sb, lnf_sb, None, stat_sb,
                                 xb_sb, qf_sb, sims_sb, t8_sb, i8u_sb, i8f_sb,
                                 cand_sb, emat_sb, f8_sb, c8u_sb, c8f_sb, oh_sb,
                                 esel_sb, eidx_sb, fsr_sb, hid_sb, coeff_sb,
                                 gdn_sb, gup_sb, acc_sb, junk_sb))
            emit_ffn(b, env, x1_out, x2_out, 0)
        with ExitStack() as dctx:
            de = dctx.enter_context
            k3_sb = de(nc.sbuf_tensor("k3_sb", [P, 4, D], BF16))
            v3_sb = de(nc.sbuf_tensor("v3_sb", [P, 16, 512], BF16))
            memq_sb = de(nc.sbuf_tensor("memq_sb", [P, 16, 6, 129], BF16))
            dpfp_sb = de(nc.sbuf_tensor("dpfp_sb", [P, 6, D], BF16))
            dptok_sb = de(nc.sbuf_tensor("dptok_sb", [P, 16, 768], BF16))
            xp_sb = de(nc.sbuf_tensor("xp_sb", [P, D], BF16))
            xn_sb = de(nc.sbuf_tensor("xn_sb", [P, D], BF16))
            rolled_sb = de(nc.sbuf_tensor("rolled_sb", [P, 2, D], BF16))
            wkv3_sb = de(nc.sbuf_tensor("wkv3_sb", [P, 16, 1024], BF16))
            bkv3_sb = de(nc.sbuf_tensor("bkv3_sb", [1, 1024], BF16))
            wmbc_sb = de(nc.sbuf_tensor("wmbc_sb", [P, 16, 512], BF16))
            bmb_sb = de(nc.sbuf_tensor("bmb_sb", [1, D], BF16))
            onesrow_bf = de(nc.sbuf_tensor("onesrow_bf", [1, P], BF16))
            mbst_sb = de(nc.sbuf_tensor("mbst_sb", [P, 512], BF16))
            mbh_sb = de(nc.sbuf_tensor("mbh_sb", [P, 16, P], BF16))
            wmv_sb = de(nc.sbuf_tensor("wmv_sb", [P, 16, P], BF16))
            coefs_sb = de(nc.sbuf_tensor("coefs_sb", [P, 16, 4], BF16))
            snorm_sb = de(nc.sbuf_tensor("snorm_sb", [P, 16], F32))
            rnorm_sb = de(nc.sbuf_tensor("rnorm_sb", [P, 16], F32))
            tchunk_sb = sq_sb[:, 0:P]
            pv_sb = sq_sb[:, P:2 * P]
            evac_sb = sq_sb[:, 2 * P:3 * P]
            dn1_sb = sq_sb[:, 3 * P:3 * P + 8]
            if DEBUG:
                pvx_sb = xp_sb.rearrange("p (a c) -> p a c", c=P)
                numx_sb = xn_sb.rearrange("p (a c) -> p a c", c=P)
                denx_sb = sq_sb[:, 1792:1824].rearrange("p (a c) -> p a c", c=2)
            ps_nd = ps_scf[0][:, 0:129]
            ps_dl = ps_scf[1][:, 0:P]
            ps_dn = ps_scf[1][:, 256:260]

            b.sync_all(G)
            b.cG.inc(G.dma_start(out=wkv3_sb[:, :, 0:512], in_=wk3T.rearrange("(f p) c -> p f c", p=P)), 16)
            b.cG.inc(G.dma_start(out=wkv3_sb[:, :, 512:1024], in_=wv3T.rearrange("(f p) c -> p f c", p=P)), 16)
            b.cG.inc(G.dma_start(out=bkv3_sb[0:1, 0:512], in_=bk3), 16)
            b.cG.inc(G.dma_start(out=bkv3_sb[0:1, 512:1024], in_=bv3), 16)
            b.cG.inc(G.dma_start(out=bmb_sb[:], in_=bmb3), 16)
            b.cG.inc(G.dma_start(out=memq_sb[:], in_=meml.rearrange("h f p c -> p h f c")), 16)
            V.tensor_copy(onesrow_bf[:], onesrow_sb[:])

            for i in range(NT_OWN):
                b.sync_all(S)
                b.cS.inc(S.dma_start(out=xb_sb[:], in_=x2_out[i * P:(i + 1) * P, :]), 16)
                b.wait(A, b.cS)
                b.wait(V, b.cS)
                emit_ln(b, xb_sb, sq_sb, stat_sb, sq_sb)
                b.wait(T, b.cA)
                for f in range(16):
                    b.cP.inc(T.transpose(ps_tr2[f % 2], sq_sb[:, f * P:(f + 1) * P], ident_f32[:]))
                    b.wait(V, b.cP)
                    V.tensor_copy(lnT_sb[:, f, :], ps_tr2[f % 2])
                    b.wait(T, b.cV)
                b.wait(T, b.cG)
                for cc in range(4):
                    pp = ps_proj[cc % 2]
                    b.sync_all(T)
                    for f in range(16):
                        T.matmul(pp, wkv3_sb[:, f, cc * P:(cc + 1) * P], lnT_sb[:, f, :],
                                 start=(f == 0), stop=False)
                    b.cP.inc(T.matmul(pp, bkv3_sb[0:1, cc * P:(cc + 1) * P],
                                      onesrow_bf[:], start=False, stop=True))
                    b.wait(V, b.cP)
                    V.tensor_copy(k3_sb[:, cc, i * P:(i + 1) * P], pp)
                    b.wait(T, b.cV)
                b.sync_all(T)
                for f in range(16):
                    T.matmul(ps_big[:, 0:512], lnT_sb[:, f, :], wkv3_sb[:, f, 512:1024],
                             start=(f == 0), stop=False)
                b.cP.inc(T.matmul(ps_big[:, 0:512], onesrow_bf[:], bkv3_sb[0:1, 512:1024],
                                  start=False, stop=True))
                b.wait(V, b.cP)
                V.tensor_copy(v3_sb[:, i, :], ps_big[:, 0:512])
                b.wait(T, b.cV)
                for cc in range(4):
                    b.sync_all(T)
                    b.sync_all(G)
                    b.cG.inc(G.dma_start(out=wmbc_sb[:],
                                         in_=wmb3T[:, cc * 512:(cc + 1) * 512]
                                         .rearrange("(f p) c -> p f c", p=P)), 16)
                    b.wait(T, b.cG)
                    for f in range(16):
                        T.matmul(ps_big[:, 0:512], lnT_sb[:, f, :], wmbc_sb[:, f, :],
                                 start=(f == 0), stop=False)
                    b.cP.inc(T.matmul(ps_big[:, 0:512], onesrow_bf[:],
                                      bmb_sb[0:1, cc * 512:(cc + 1) * 512],
                                      start=False, stop=True))
                    b.wait(A, b.cP)
                    b.wait(A, b.cS)
                    A.activation(out=mbst_sb[:], in_=ps_big[:, 0:512], func=AF.Sigmoid)
                    b.sync_all(S)
                    b.cS.inc(S.dma_start(out=mb3_dram[i * P:(i + 1) * P, cc * 512:(cc + 1) * 512],
                                         in_=mbst_sb[:]), 16)
            for hp in range(4):
                b.sync_all(V)
                V.memset(snorm_sb[:], 0.0)
                V.tensor_scalar_max(xp_sb[:], k3_sb[:, hp, :], 0.0)
                V.tensor_scalar(xn_sb[:], k3_sb[:, hp, :], -1.0, 0.0, op0=ALU.mult, op1=ALU.max)
                for j in (1, 2, 3):
                    b.sync_all(S)
                    b.cS.inc(S.dma_start(out=rolled_sb[0:j, 0, :], in_=xn_sb[P - j:P, :]), 16)
                    b.cS.inc(S.dma_start(out=rolled_sb[j:P, 0, :], in_=xp_sb[0:P - j, :]), 16)
                    b.cS.inc(S.dma_start(out=rolled_sb[0:j, 1, :], in_=xp_sb[P - j:P, :]), 16)
                    b.cS.inc(S.dma_start(out=rolled_sb[j:P, 1, :], in_=xn_sb[0:P - j, :]), 16)
                    b.wait(V, b.cS)
                    V.tensor_mul(dpfp_sb[:, 2 * (j - 1), :], xp_sb[:], rolled_sb[:, 0, :])
                    V.tensor_mul(dpfp_sb[:, 2 * (j - 1) + 1, :], xn_sb[:], rolled_sb[:, 1, :])
                for tt in range(NT_OWN):
                    for fc in range(6):
                        V.tensor_copy(tchunk_sb, dpfp_sb[:, fc, tt * P:(tt + 1) * P])
                        b.wait(T, b.cV)
                        b.cP.inc(T.transpose(ps_tr2[fc % 2], tchunk_sb, ident_f32[:]))
                        b.wait(V, b.cP)
                        V.tensor_copy(dptok_sb[:, tt, fc * P:(fc + 1) * P], ps_tr2[fc % 2])
                        b.wait(T, b.cV)
                    b.wait(A, b.cV)
                    A.activation(out=sq_sb[:, 1024:1792], in_=dptok_sb[:, tt, :], func=AF.Square,
                                 accum_out=snorm_sb[:, tt:tt + 1])
                b.wait(A, b.cA)
                A.activation(out=rnorm_sb[:], in_=snorm_sb[:], func=AF.Ln)
                A.activation(out=snorm_sb[:], in_=rnorm_sb[:], func=AF.Exp, scale=0.5)
                b.wait(V, b.cA)
                V.tensor_scalar_max(snorm_sb[:], snorm_sb[:], 1e-12)
                V.tensor_copy(rnorm_sb[:], snorm_sb[:])
                V.reciprocal(rnorm_sb[:], rnorm_sb[:])
                for r in range(4):
                    h = 4 * hp + r
                    b.sync_all(S)
                    b.cS.inc(S.dma_start(out=mbh_sb[:],
                                         in_=mb3_dram[:, h * P:(h + 1) * P]
                                         .rearrange("(a p) c -> p a c", p=P)), 16)
                    b.wait(V, b.cS)
                    for tt in range(NT_OWN):
                        b.sync_all(T)
                        for fc in range(6):
                            mmi = T.matmul(ps_nd, dpfp_sb[:, fc, tt * P:(tt + 1) * P],
                                           memq_sb[:, h, fc, :], start=(fc == 0), stop=(fc == 5))
                        b.cP.inc(mmi)
                        b.wait(V, b.cP)
                        V.scalar_tensor_tensor(out=stat_sb[:, 0:1], in0=snorm_sb[:, tt:tt + 1],
                                               scalar=1e-8, in1=ps_nd[:, 128:129],
                                               op0=ALU.mult, op1=ALU.add)
                        V.reciprocal(stat_sb[:, 1:2], stat_sb[:, 0:1])
                        V.tensor_scalar(pv_sb, ps_nd[:, 0:P], stat_sb[:, 1:2], None, op0=ALU.mult)
                        if DEBUG and hp == 0 and r == 3:
                            V.tensor_copy(pvx_sb[:, tt, :], pv_sb)
                            V.tensor_copy(numx_sb[:, tt, :], ps_nd[:, 0:P])
                            V.tensor_copy(denx_sb[:, tt, 0:1], ps_nd[:, 128:129])
                            V.tensor_copy(denx_sb[:, tt, 1:2], stat_sb[:, 0:1])
                        V.scalar_tensor_tensor(out=stat_sb[:, 2:3], in0=ps_nd[:, 128:129],
                                               scalar=1.0, in1=rnorm_sb[:, tt:tt + 1],
                                               op0=ALU.mult, op1=ALU.mult)
                        V.tensor_scalar(stat_sb[:, 2:3], stat_sb[:, 2:3], -1.0, 1.0 - 1e-8,
                                        op0=ALU.mult, op1=ALU.add)
                        V.tensor_scalar_max(stat_sb[:, 2:3], stat_sb[:, 2:3], 0.0)
                        V.tensor_scalar_min(stat_sb[:, 2:3], stat_sb[:, 2:3], 1.0)
                        V.tensor_mul(coefs_sb[:, tt, r:r + 1], stat_sb[:, 2:3], rnorm_sb[:, tt:tt + 1])
                        V.tensor_sub(pv_sb, v3_sb[:, tt, hp * P:(hp + 1) * P], pv_sb)
                        V.tensor_scalar(pv_sb, pv_sb, rnorm_sb[:, tt:tt + 1], None, op0=ALU.mult)
                        V.tensor_mul(wmv_sb[:, tt, :], pv_sb, mbh_sb[:, tt, :])
                    for fc in range(6):
                        b.sync_all(T)
                        for tt in range(NT_OWN):
                            mmi = T.matmul(ps_dl, dptok_sb[:, tt, fc * P:(fc + 1) * P],
                                           wmv_sb[:, tt, :], start=(tt == 0), stop=(tt == 15))
                        b.cP.inc(mmi)
                        b.wait(V, b.cP)
                        b.wait(V, b.cS)
                        V.tensor_copy(evac_sb, ps_dl)
                        b.sync_all(S)
                        b.cS.inc(S.dma_start(out=delta_out[h, fc], in_=evac_sb), 16)
                if DEBUG and hp == 0:
                    d = dbg_b
                    b.sync_all(S)
                    b.cS.inc(S.dma_start(out=d["pvall"], in_=pvx_sb[:]), 16)
                    b.cS.inc(S.dma_start(out=d["mbh"], in_=mbh_sb[:]), 16)
                    b.cS.inc(S.dma_start(out=d["numall"], in_=numx_sb[:]), 16)
                    b.cS.inc(S.dma_start(out=d["denall"], in_=denx_sb[:]), 16)
                    b.cS.inc(S.dma_start(out=d["dptok"], in_=dptok_sb[:]), 16)
                    b.cS.inc(S.dma_start(out=d["snorm"], in_=snorm_sb[:]), 16)
                    b.cS.inc(S.dma_start(out=d["rnorm"], in_=rnorm_sb[:]), 16)
                    b.cS.inc(S.dma_start(out=d["wmv"], in_=wmv_sb[:]), 16)
                    b.cS.inc(S.dma_start(out=d["coefs"], in_=coefs_sb[:]), 16)
                    b.cS.inc(S.dma_start(out=d["dpfp"], in_=dpfp_sb[:]), 16)
                    b.cS.inc(S.dma_start(out=d["k3"], in_=k3_sb[:]), 16)
                    b.cS.inc(S.dma_start(out=d["v3"], in_=v3_sb[:]), 16)
                b.wait(V, b.cV)
                for fc in range(6):
                    b.sync_all(T)
                    for tt in range(NT_OWN):
                        mmi = T.matmul(ps_dn, dptok_sb[:, tt, fc * P:(fc + 1) * P],
                                       coefs_sb[:, tt, :], start=(tt == 0), stop=(tt == 15))
                    b.cP.inc(mmi)
                    b.wait(V, b.cP)
                    b.wait(V, b.cS)
                    V.tensor_copy(dn1_sb[:, 0:4], ps_dn)
                    b.sync_all(S)
                    b.cS.inc(S.dma_start(out=dnorm_out[fc, :, 4 * hp:4 * hp + 4],
                                         in_=dn1_sb[:, 0:4]), 16)
        b.sync_all(S)
        b.sync_all(G)
    return nc


def build_l2():
    nc = bass.Bass("TRN2", target_bir_lowering=False)
    dt = nc.dram_tensor
    x2in = dt("x2in", [NT_OWN * P, D], F32, kind="ExternalInput").ap()
    meml = dt("meml", [16, 6, P, 129], F32, kind="ExternalInput").ap()
    delta_all = dt("delta_all", [16, 6, P, 8, P], F32, kind="ExternalInput").ap()
    dnorm_all = dt("dnorm_all", [6, P, 16, 8], F32, kind="ExternalInput").ap()
    wq3T = dt("wq3T", [D, D], F32, kind="ExternalInput").ap()
    bq3 = dt("bq3", [1, D], F32, kind="ExternalInput").ap()
    wg3T = dt("wg3T", [D, D], F32, kind="ExternalInput").ap()
    bg3 = dt("bg3", [1, D], F32, kind="ExternalInput").ap()
    wqf = dt("wqf", [D, 512], F32, kind="ExternalInput").ap()
    keysT = dt("keysT", [64, 8, P], F32, kind="ExternalInput").ap()
    e_down = dt("e_down", [16384, D], F32, kind="ExternalInput").ap()
    e_up = dt("e_up", [16384, D], F32, kind="ExternalInput").ap()
    iota64 = dt("iota64", [P, 64], F32, kind="ExternalInput").ap()
    identity32 = dt("identity32", [P, P], F32, kind="ExternalInput").ap()
    onescol = dt("onescol", [P, 1], F32, kind="ExternalInput").ap()
    onesrow = dt("onesrow", [1, P], F32, kind="ExternalInput").ap()
    y_out = dt("y_out", [NT_OWN * P, D], F32, kind="ExternalOutput").ap()
    dk = "ExternalOutput" if DEBUG else "Internal"
    lnf_dram = dt("lnf_dram", [NT_OWN * P, D], F32, kind=dk).ap()
    lnT3_dram = dt("lnT3_dram", [16, P, NT_OWN * P], BF16, kind=dk).ap()
    q3_dram = dt("q3_dram", [16, P, NT_OWN * P], BF16, kind=dk).ap()
    mh_dram = dt("mh_dram", [NT_OWN * P, D], BF16, kind=dk).ap()
    x3_dram = dt("x3_dram", [NT_OWN * P, D], F32, kind=dk).ap()
    if DEBUG:
        dbg_memq = dt("dbg_memq", [P, 16, 6, 129], BF16, kind="ExternalOutput").ap()
        dbg_gate = dt("dbg_gate", [NT_OWN * P, D], BF16, kind="ExternalOutput").ap()

    with ExitStack() as ctx:
        b = B(nc, ctx)
        T, V, A, S, G = nc.tensor, b.V, b.A, nc.sync, nc.gpsimd

        ident_f32 = b.sb("ident_f32", [P, P], F32)
        ones_sb = b.sb("ones_sb", [P, 1], F32)
        onesbf_sb = b.sb("onesbf_sb", [P, 1], BF16)
        onesrow_sb = b.sb("onesrow_sb", [1, P], F32)
        onesrow_bf = b.sb("onesrow_bf", [1, P], BF16)
        xb_sb = b.sb("xb", [P, D], F32)
        sq_sb = b.sb("sq_sb", [P, D], F32)
        stat_sb = b.sb("stat_sb", [P, 8], F32)
        lnT_sb = b.sb("lnT_sb", [P, 16, P], BF16)
        memq_sb = b.sb("memq_sb", [P, 16, 6, 129], BF16)

        ps_scf = [b.ps(f"ps_scf{i}", [P, 1536], F32) for i in range(2)]
        ps_midf = b.ps("ps_midf", [P, 512], F32)
        ps_tr2 = [ps_midf[:, 0:128], ps_midf[:, 128:256]]
        ps_proj = [ps_midf[:, 256:384], ps_midf[:, 384:512]]
        ps_big = b.ps("ps_big", [P, 512], F32)
        ps_nd = ps_scf[0][:, 0:129]
        ps_nrm1 = ps_scf[1][:, 0:1]

        for dst, srcap in ((ident_f32, identity32), (ones_sb, onescol),
                           (onesrow_sb, onesrow)):
            b.cS.inc(S.dma_start(out=dst[:], in_=srcap), 16)
        b.wait(V, b.cS)
        V.tensor_copy(onesbf_sb[:], ones_sb[:])
        V.tensor_copy(onesrow_bf[:], onesrow_sb[:])

        with ExitStack() as ectx:
            ee = ectx.enter_context
            mpc_sb = ee(nc.sbuf_tensor("mpc_sb", [P, 129], F32))
            dpc_sb = ee(nc.sbuf_tensor("dpc_sb", [P, 8, P], F32))
            npc_sb = ee(nc.sbuf_tensor("npc_sb", [P, 8], F32))
            for h in range(16):
                for fc in range(6):
                    b.sync_all(S)
                    b.cS.inc(S.dma_start(out=mpc_sb[:], in_=meml[h, fc]), 16)
                    b.cS.inc(S.dma_start(out=dpc_sb[:], in_=delta_all[h, fc].rearrange("p c q -> p c q")), 16)
                    b.cS.inc(S.dma_start(out=npc_sb[:], in_=dnorm_all[fc, :, h, :]), 16)
                    b.wait(V, b.cS)
                    for c in range(8):
                        V.tensor_add(mpc_sb[:, 0:P], mpc_sb[:, 0:P], dpc_sb[:, c, :])
                        V.tensor_add(mpc_sb[:, 128:129], mpc_sb[:, 128:129], npc_sb[:, c:c + 1])
                    V.tensor_copy(memq_sb[:, h, fc, :], mpc_sb[:])
            if DEBUG:
                b.sync_all(S)
                b.cS.inc(S.dma_start(out=dbg_memq, in_=memq_sb[:]), 16)

        with ExitStack() as qctx:
            qe = qctx.enter_context
            wq_sb = qe(nc.sbuf_tensor("wq_sb", [P, 16, 16, P], BF16))
            bq_sb = qe(nc.sbuf_tensor("bq_sb", [1, D], BF16))
            qall_sb = qe(nc.sbuf_tensor("qall_sb", [P, 16, P], BF16))
            b.sync_all(G)
            b.cG.inc(G.dma_start(out=wq_sb[:],
                                 in_=wq3T.rearrange("(f p) (m c) -> p f m c", p=P, c=P)), 16)
            b.cG.inc(G.dma_start(out=bq_sb[:], in_=bq3), 16)
            for i in range(NT_OWN):
                b.sync_all(S)
                b.cS.inc(S.dma_start(out=xb_sb[:], in_=x2in[i * P:(i + 1) * P, :]), 16)
                b.wait(A, b.cS)
                b.wait(V, b.cS)
                emit_ln(b, xb_sb, sq_sb, stat_sb, sq_sb)
                b.sync_all(S)
                b.cS.inc(S.dma_start(out=lnf_dram[i * P:(i + 1) * P, :], in_=sq_sb[:]), 16)
                b.wait(T, b.cA)
                for f in range(16):
                    b.cP.inc(T.transpose(ps_tr2[f % 2], sq_sb[:, f * P:(f + 1) * P], ident_f32[:]))
                    b.wait(V, b.cP)
                    V.tensor_copy(lnT_sb[:, f, :], ps_tr2[f % 2])
                    b.wait(T, b.cV)
                b.wait(S, b.cV)
                b.cS.inc(S.dma_start(out=lnT3_dram[:, :, i * P:(i + 1) * P].rearrange("f p q -> p f q"),
                                     in_=lnT_sb[:]), 16)
                b.wait(T, b.cG)
                for m in range(16):
                    pp = ps_proj[m % 2]
                    b.sync_all(T)
                    for f in range(16):
                        T.matmul(pp, wq_sb[:, f, m, :], lnT_sb[:, f, :],
                                 start=(f == 0), stop=False)
                    b.cP.inc(T.matmul(pp, bq_sb[0:1, m * P:(m + 1) * P], onesrow_bf[:],
                                      start=False, stop=True))
                    b.wait(V, b.cP)
                    V.tensor_copy(qall_sb[:, m, :], pp)
                    b.wait(T, b.cV)
                b.sync_all(S)
                b.cS.inc(S.dma_start(out=q3_dram[:, :, i * P:(i + 1) * P].rearrange("m p q -> p m q"),
                                     in_=qall_sb[:]), 16)

        with ExitStack() as hctx:
            he = hctx.enter_context
            qh_sb = he(nc.sbuf_tensor("qh_sb", [P, D], BF16))
            dpfp_sb = he(nc.sbuf_tensor("dpfp_sb", [P, 6, D], BF16))
            xp_sb = he(nc.sbuf_tensor("xp_sb", [P, D], BF16))
            xn_sb = he(nc.sbuf_tensor("xn_sb", [P, D], BF16))
            rolled_sb = he(nc.sbuf_tensor("rolled_sb", [P, 2, D], BF16))
            sqb_sb = he(nc.sbuf_tensor("sqb_sb", [P, D], BF16))
            snorm_sb = he(nc.sbuf_tensor("snorm_sb", [P, 16], F32))
            mhh_sb = he(nc.sbuf_tensor("mhh_sb", [P, 16, P], BF16))
            for h in range(16):
                b.sync_all(S)
                b.cS.inc(S.dma_start(out=qh_sb[:], in_=q3_dram[h]), 16)
                b.sync_all(V)
                b.sync_all(T)
                V.memset(snorm_sb[:], 0.0)
                V.tensor_scalar_max(xp_sb[:], qh_sb[:], 0.0)
                V.tensor_scalar(xn_sb[:], qh_sb[:], -1.0, 0.0, op0=ALU.mult, op1=ALU.max)
                for j in (1, 2, 3):
                    b.sync_all(S)
                    b.cS.inc(S.dma_start(out=rolled_sb[0:j, 0, :], in_=xn_sb[P - j:P, :]), 16)
                    b.cS.inc(S.dma_start(out=rolled_sb[j:P, 0, :], in_=xp_sb[0:P - j, :]), 16)
                    b.cS.inc(S.dma_start(out=rolled_sb[0:j, 1, :], in_=xp_sb[P - j:P, :]), 16)
                    b.cS.inc(S.dma_start(out=rolled_sb[j:P, 1, :], in_=xn_sb[0:P - j, :]), 16)
                    b.wait(V, b.cS)
                    V.tensor_mul(dpfp_sb[:, 2 * (j - 1), :], xp_sb[:], rolled_sb[:, 0, :])
                    V.tensor_mul(dpfp_sb[:, 2 * (j - 1) + 1, :], xn_sb[:], rolled_sb[:, 1, :])
                for fc in range(6):
                    b.wait(A, b.cV)
                    A.activation(out=sqb_sb[:], in_=dpfp_sb[:, fc, :], func=AF.Square)
                    b.wait(T, b.cA)
                    for tt in range(NT_OWN):
                        b.cP.inc(T.matmul(ps_scf[1][:, tt:tt + 1], sqb_sb[:, tt * P:(tt + 1) * P],
                                          onesbf_sb[:], start=(fc == 0), stop=(fc == 5)))
                b.wait(V, b.cP)
                V.tensor_copy(snorm_sb[:], ps_scf[1][:, 0:16])
                b.wait(A, b.cV)
                A.activation(out=sq_sb[:, 0:16], in_=snorm_sb[:], func=AF.Ln)
                A.activation(out=snorm_sb[:], in_=sq_sb[:, 0:16], func=AF.Exp, scale=0.5)
                b.wait(V, b.cA)
                V.tensor_scalar_max(snorm_sb[:], snorm_sb[:], 1e-12)
                for tt in range(NT_OWN):
                    b.sync_all(T)
                    for fc in range(6):
                        mmi = T.matmul(ps_nd, dpfp_sb[:, fc, tt * P:(tt + 1) * P],
                                       memq_sb[:, h, fc, :], start=(fc == 0), stop=(fc == 5))
                    b.cP.inc(mmi)
                    b.wait(V, b.cP)
                    V.scalar_tensor_tensor(out=stat_sb[:, 0:1], in0=snorm_sb[:, tt:tt + 1],
                                           scalar=1e-8, in1=ps_nd[:, 128:129],
                                           op0=ALU.mult, op1=ALU.add)
                    V.reciprocal(stat_sb[:, 1:2], stat_sb[:, 0:1])
                    V.tensor_scalar(mhh_sb[:, tt, :], ps_nd[:, 0:P], stat_sb[:, 1:2],
                                    None, op0=ALU.mult)
                b.sync_all(S)
                b.cS.inc(S.dma_start(out=mh_dram[:, h * P:(h + 1) * P]
                                     .rearrange("(a p) c -> p a c", p=P), in_=mhh_sb[:]), 16)

        with ExitStack() as gctx:
            ge = gctx.enter_context
            wg_sb = ge(nc.sbuf_tensor("wg_sb", [P, 16, 512], BF16))
            bg_sb = ge(nc.sbuf_tensor("bg_sb", [1, D], BF16))
            gst_sb = ge(nc.sbuf_tensor("gst_sb", [P, 512], BF16))
            mht_sb = ge(nc.sbuf_tensor("mht_sb", [P, D], BF16))
            lnx_sb = ge(nc.sbuf_tensor("lnx_sb", [P, D], F32))
            gate_sb = ge(nc.sbuf_tensor("gate_sb", [P, D], BF16))
            b.sync_all(G)
            b.cG.inc(G.dma_start(out=bg_sb[:], in_=bg3), 16)
            for i in range(NT_OWN):
                b.sync_all(S)
                b.cS.inc(S.dma_start(out=lnT_sb[:],
                                     in_=lnT3_dram[:, :, i * P:(i + 1) * P].rearrange("f p q -> p f q")), 16)
                b.cS.inc(S.dma_start(out=xb_sb[:], in_=x2in[i * P:(i + 1) * P, :]), 16)
                b.cS.inc(S.dma_start(out=lnx_sb[:], in_=lnf_dram[i * P:(i + 1) * P, :]), 16)
                b.cS.inc(S.dma_start(out=mht_sb[:], in_=mh_dram[i * P:(i + 1) * P, :]), 16)
                for cc in range(4):
                    b.sync_all(T)
                    b.sync_all(G)
                    b.cG.inc(G.dma_start(out=wg_sb[:],
                                         in_=wg3T[:, cc * 512:(cc + 1) * 512]
                                         .rearrange("(f p) c -> p f c", p=P)), 16)
                    b.wait(T, b.cG)
                    b.wait(T, b.cS)
                    for f in range(16):
                        T.matmul(ps_big[:, 0:512], lnT_sb[:, f, :], wg_sb[:, f, :],
                                 start=(f == 0), stop=False)
                    b.cP.inc(T.matmul(ps_big[:, 0:512], onesrow_bf[:],
                                      bg_sb[0:1, cc * 512:(cc + 1) * 512],
                                      start=False, stop=True))
                    b.wait(A, b.cP)
                    b.wait(A, b.cV)
                    A.activation(out=gate_sb[:, cc * 512:(cc + 1) * 512],
                                 in_=ps_big[:, 0:512], func=AF.Sigmoid)
                if DEBUG:
                    b.sync_all(S)
                    b.cS.inc(S.dma_start(out=dbg_gate[i * P:(i + 1) * P, :], in_=gate_sb[:]), 16)
                b.wait(V, b.cA)
                b.wait(V, b.cS)
                V.tensor_add(sq_sb[:], xb_sb[:], lnx_sb[:])
                V.tensor_mul(lnx_sb[:], gate_sb[:], mht_sb[:])
                V.tensor_add(sq_sb[:], sq_sb[:], lnx_sb[:])
                b.sync_all(S)
                b.cS.inc(S.dma_start(out=x3_dram[i * P:(i + 1) * P, :], in_=sq_sb[:]), 16)

        with ExitStack() as fctx:
            fe = fctx.enter_context
            wqf_sb = fe(nc.sbuf_tensor("wqf_sb", [P, 16, 512], BF16))
            keys_sb = fe(nc.sbuf_tensor("keys_sb", [P, 8, P], BF16))
            iota64_sb = fe(nc.sbuf_tensor("iota64_sb", [P, 64], F32))
            lnf_sb = fe(nc.sbuf_tensor("lnf_sb", [P, D], F32))
            qf_sb = fe(nc.sbuf_tensor("qf_sb", [P, 4, P], BF16))
            sims_sb = fe(nc.sbuf_tensor("sims_sb", [P, 8, P], F32))
            t8_sb = fe(nc.sbuf_tensor("t8_sb", [P, 8, 8], F32))
            i8u_sb = fe(nc.sbuf_tensor("i8u_sb", [P, 8, 8], U32))
            i8f_sb = fe(nc.sbuf_tensor("i8f_sb", [P, 8, 8], F32))
            cand_sb = fe(nc.sbuf_tensor("cand_sb", [P, 64], F32))
            emat_sb = fe(nc.sbuf_tensor("emat_sb", [P, 64], F32))
            f8_sb = fe(nc.sbuf_tensor("f8_sb", [P, 4, 8], F32))
            c8u_sb = fe(nc.sbuf_tensor("c8u_sb", [P, 8], U32))
            c8f_sb = fe(nc.sbuf_tensor("c8f_sb", [P, 8], F32))
            oh_sb = fe(nc.sbuf_tensor("oh_sb", [P, 64], F32))
            esel_sb = fe(nc.sbuf_tensor("esel_sb", [P, 32], F32))
            eidx_sb = fe(nc.sbuf_tensor("eidx_sb", [P, 32], U32))
            fsr_sb = fe(nc.sbuf_tensor("fsr_sb", [P, 4, 8], F32))
            hid_sb = fe(nc.sbuf_tensor("hid_sb", [P, 32], F32))
            coeff_sb = fe(nc.sbuf_tensor("coeff_sb", [P, 32], F32))
            gdn_sb = fe(nc.sbuf_tensor("gdn_sb", [P, D], F32))
            gup_sb = fe(nc.sbuf_tensor("gup_sb", [P, D], F32))
            acc_sb = fe(nc.sbuf_tensor("acc_sb", [P, D], F32))
            junk_sb = fe(nc.sbuf_tensor("junk_sb", [P, 72], F32))
            junk2_sb = fe(nc.sbuf_tensor("junk2_sb", [P, D], F32))
            b.sync_all(G)
            b.cG.inc(G.dma_start(out=wqf_sb[:], in_=wqf.rearrange("(f p) c -> p f c", p=P)), 16)
            b.cG.inc(G.dma_start(out=keys_sb[0:64, :, :], in_=keysT), 16)
            b.cG.inc(G.dma_start(out=keys_sb[64:128, :, :], in_=keysT), 16)
            b.cS.inc(S.dma_start(out=iota64_sb[:], in_=iota64), 16)
            b.wait(T, b.cG)
            b.wait(V, b.cS)
            env = dict(dbg=None,
                       ps_sims=[ps_scf[0][:, 0:P], ps_scf[1][:, 0:P]],
                       T=T, V=V, A=A, S=S, G=G, ident_f32=ident_f32, lnT_sb=lnT_sb,
                       ps_proj=ps_proj, ps_tr2=ps_tr2, e_down=e_down, e_up=e_up,
                       junk2=junk2_sb,
                       ffn_bufs=(wqf_sb, keys_sb, iota64_sb, lnf_sb, None, stat_sb,
                                 xb_sb, qf_sb, sims_sb, t8_sb, i8u_sb, i8f_sb,
                                 cand_sb, emat_sb, f8_sb, c8u_sb, c8f_sb, oh_sb,
                                 esel_sb, eidx_sb, fsr_sb, hid_sb, coeff_sb,
                                 gdn_sb, gup_sb, acc_sb, junk_sb))
            emit_ffn(b, env, x3_dram, y_out, 0)
        b.sync_all(S)
        b.sync_all(G)
    return nc


def _consts(core):
    half = core % 2
    off = 2048 * half
    ident = np.eye(P).astype(ml_dtypes.bfloat16)
    ident32 = np.eye(P).astype(np.float32)
    pp = np.arange(P)[:, None]
    ff = np.arange(P)[None, :]
    tri0 = (pp > ff).astype(ml_dtypes.bfloat16)
    tri8 = (pp <= ff).astype(ml_dtypes.bfloat16)
    slot = (np.arange(NT_CTX)[None, :] * P + pp).astype(np.float32)
    j_abs = off - HALO + slot
    cmask = (j_abs >= 0).astype(np.float32) if half == 0 else np.ones((P, NT_CTX), np.float32)
    iotaJ = j_abs.astype(np.float32)
    iotaI = (off + np.arange(NT_OWN)[None, :] * P + pp).astype(np.float32)
    onescol = np.ones((P, 1), np.float32)
    onesrow = np.ones((1, P), np.float32)
    return dict(identity=ident, identity32=ident32, tri0=tri0, tri8=tri8, cmask_in=cmask,
                iotaJ=iotaJ, iotaI=iotaI, onescol=onescol, onesrow=onesrow)


def kernel(**inputs):
    x = np.asarray(inputs["L0_attn_a_hidden_states"])
    cu = np.asarray(inputs["L0_attn_a_cu_seqlens_q"]).astype(np.int32)
    Wq = np.asarray(inputs["L0_attn_a_Wq"])
    Wk = np.asarray(inputs["L0_attn_a_Wk"])
    Wv = np.asarray(inputs["L0_attn_a_Wv"])
    Wo = np.asarray(inputs["L0_attn_a_Wo"])

    wq_perm = Wq.reshape(32, 64, D)[HEAD_ORDER].reshape(D, D)
    wqT = np.ascontiguousarray(wq_perm.T)
    wkT = np.ascontiguousarray(Wk.T)
    wvT = np.ascontiguousarray(Wv.T)
    woT = np.ascontiguousarray(Wo.T.reshape(32, 64, D)[HEAD_ORDER].reshape(D, D))
    cuq = np.zeros((1, 4), np.int32)
    cuq[0, :3] = cu
    Wqf = np.ascontiguousarray(np.asarray(inputs["L0_ffn_Wq"]))
    keysT = np.ascontiguousarray(np.asarray(inputs["L0_ffn_keys"]).transpose(3, 0, 2, 1)
                                 .reshape(64, 8, P))
    e_down_np = np.ascontiguousarray(np.asarray(inputs["L0_ffn_e_down"]))
    e_up_np = np.ascontiguousarray(np.asarray(inputs["L0_ffn_e_up"]))
    iota64_np = np.broadcast_to(np.arange(64, dtype=np.float32), (P, 64)).copy()
    wk3T = np.ascontiguousarray(np.asarray(inputs["L0_attn_b_Wk"]).T)
    bk3 = np.asarray(inputs["L0_attn_b_bk"]).reshape(1, 512)
    wv3T = np.ascontiguousarray(np.asarray(inputs["L0_attn_b_Wv"]).T)
    bv3 = np.asarray(inputs["L0_attn_b_bv"]).reshape(1, 512)
    wmb3T = np.ascontiguousarray(np.asarray(inputs["L0_attn_b_Wmb"]).T)
    bmb3 = np.asarray(inputs["L0_attn_b_bmb"]).reshape(1, D)
    mem_np = np.asarray(inputs["L0_attn_b_memory"]).reshape(16, 6, P, P)
    norm_np = np.asarray(inputs["L0_attn_b_norm"]).reshape(16, 6, P, 1)
    meml_np = np.ascontiguousarray(np.concatenate([mem_np, norm_np], axis=3))

    in_maps = []
    for c in range(8):
        bb, half = c // 2, c % 2
        own = x[bb, half * 2048:(half + 1) * 2048]
        if half == 0:
            halo = np.zeros((HALO, D), np.float32)
        else:
            halo = x[bb, 2048 - HALO:2048]
        x0h = np.ascontiguousarray(np.concatenate([halo, own], 0))
        m = dict(x0h=x0h, wqT=wqT, wkT=wkT, wvT=wvT, woT=woT, cuq=cuq,
                 wqf=Wqf, keysT=keysT, e_down=e_down_np, e_up=e_up_np,
                 iota64=iota64_np, wk3T=wk3T, bk3=bk3, wv3T=wv3T, bv3=bv3,
                 wmb3T=wmb3T, bmb3=bmb3, meml=meml_np)
        m.update(_consts(c))
        in_maps.append(m)

    nc = build_l1()
    res = run_bass_kernel_spmd(nc, in_maps, core_ids=list(range(8)))
    if OUT_KEY != "y_out":
        out = np.zeros((4, 4096, D), np.float32)
        for c in range(8):
            bb, half = c // 2, c % 2
            out[bb, half * 2048:(half + 1) * 2048] = res.results[c][OUT_KEY]
        return out

    delta_all = np.stack([res.results[c]["delta_out"] for c in range(8)], axis=3)
    dnorm_all = np.stack([res.results[c]["dnorm_out"] for c in range(8)], axis=3)
    wq3T = np.ascontiguousarray(np.asarray(inputs["L0_attn_b_Wq"]).T)
    bq3 = np.asarray(inputs["L0_attn_b_bq"]).reshape(1, D)
    wg3T = np.ascontiguousarray(np.asarray(inputs["L0_attn_b_Wg"]).T)
    bg3 = np.asarray(inputs["L0_attn_b_bg"]).reshape(1, D)
    ident32 = np.eye(P).astype(np.float32)
    onescol = np.ones((P, 1), np.float32)
    onesrow = np.ones((1, P), np.float32)
    in_maps2 = []
    for c in range(8):
        m2 = dict(x2in=np.ascontiguousarray(res.results[c]["x2_out"]),
                  meml=meml_np, delta_all=delta_all, dnorm_all=dnorm_all,
                  wq3T=wq3T, bq3=bq3, wg3T=wg3T, bg3=bg3,
                  wqf=Wqf, keysT=keysT, e_down=e_down_np, e_up=e_up_np,
                  iota64=iota64_np, identity32=ident32,
                  onescol=onescol, onesrow=onesrow)
        in_maps2.append(m2)
    nc2 = build_l2()
    res2 = run_bass_kernel_spmd(nc2, in_maps2, core_ids=list(range(8)))
    out = np.zeros((4, 4096, D), np.float32)
    for c in range(8):
        bb, half = c // 2, c % 2
        out[bb, half * 2048:(half + 1) * 2048] = res2.results[c]["y_out"]
    return out
